# version 12
# baseline (speedup 1.0000x reference)
import contextlib
import numpy as np
import ml_dtypes
import concourse.bass as bass
import concourse.mybir as mybir
from concourse.bass_utils import run_bass_kernel_spmd

F32 = mybir.dt.float32
BF16 = mybir.dt.bfloat16
I8 = mybir.dt.int8
AF = mybir.ActivationFunctionType
ALU = mybir.AluOpType
AXX = mybir.AxisListType.X

D = 2048
NT = 8
FC = 16
MAGIC = 1.5 * 2.0 ** 23
EPS_LN = 1e-5
EPS_RMS = 1e-6
HID = 8192
NH = 16
NEG = -1e30
ENGINES = ("sync", "tensor", "vector", "scalar", "gpsimd")


class Sem:
    __slots__ = ("h", "n")

    def __init__(self, nc, name):
        self.h = nc.alloc_semaphore(name=name)
        self.n = 0

    def cur(self):
        return (self, self.n)


class Ph:

    def __init__(self, nc):
        self.nc = nc
        self.q = {e: [] for e in ENGINES}

    def o(self, eng, fn, w=(), i=()):
        ws = tuple((s.h, v) for (s, v) in w)
        incs = []
        for (s, k) in i:
            s.n += k
            incs.append((s.h, k))
        self.q[eng].append((fn, ws, tuple(incs)))

    def close(self):
        with self.nc.Block() as block:
            for e in ENGINES:
                items = self.q[e]
                if not items:
                    continue

                def body(eng, items=items):
                    for fn, ws, incs in items:
                        for h, v in ws:
                            eng.wait_ge(h, v)
                        inst = fn(eng)
                        for h, k in incs:
                            inst.then_inc(h, k)

                getattr(block, e)(body)
        self.q = None


def build_program(n_sub=4):
    nc = bass.Bass("TRN2", target_bir_lowering=False)

    x_own = nc.dram_tensor("x_own", [1024, D], F32, kind="ExternalInput")
    x_prev = nc.dram_tensor("x_prev", [512, D], F32, kind="ExternalInput")
    pmrow_in = nc.dram_tensor("pmrow", [1, 512], F32, kind="ExternalInput")
    agw_e = nc.dram_tensor("agw_e", [128, 3], F32, kind="ExternalInput")
    agw_m = nc.dram_tensor("agw_m", [128, 3], F32, kind="ExternalInput")
    wq_d = nc.dram_tensor("wq", [4, 128, FC, 512], F32, kind="ExternalInput")
    wk_d = nc.dram_tensor("wk", [1, 128, FC, 512], F32, kind="ExternalInput")
    wv_d = nc.dram_tensor("wv", [1, 128, FC, 512], F32, kind="ExternalInput")
    wo_d = nc.dram_tensor("wo", [4, 128, FC, 512], F32, kind="ExternalInput")
    wqkv_d = nc.dram_tensor("wqkv_r", [12, 128, FC, 512], F32, kind="ExternalInput")
    wg_d = nc.dram_tensor("wg_t", [4, 128, FC, 512], F32, kind="ExternalInput")
    wo2_d = nc.dram_tensor("wout2", [4, 128, FC, 512], F32, kind="ExternalInput")
    normw_in = nc.dram_tensor("normw", [2048], F32, kind="ExternalInput")
    slope_in = nc.dram_tensor("slope", [128, 16], F32, kind="ExternalInput")
    ob_r_in = nc.dram_tensor("ob_r", [128, D], F32, kind="ExternalInput")
    wg_sh = nc.dram_tensor("wg_sh", [2, 128, FC, 512], F32, kind="ExternalInput")
    wv_sh = nc.dram_tensor("wv_sh", [2, 128, FC, 512], F32, kind="ExternalInput")
    wo_sh = nc.dram_tensor("wo_sh", [32, 128, 512], F32, kind="ExternalInput")
    gb_d = nc.dram_tensor("gate_b", [HID], F32, kind="ExternalInput")
    vb_d = nc.dram_tensor("val_b", [HID], F32, kind="ExternalInput")
    ob_d = nc.dram_tensor("out_b", [D], F32, kind="ExternalInput")
    y_out = nc.dram_tensor("y", [1024, D], F32, kind="ExternalOutput")

    tern_g = nc.dram_tensor("tern_g", [16, 128, FC, 512], BF16, addr_space="Shared")
    tern_v = nc.dram_tensor("tern_v", [16, 128, FC, 512], BF16, addr_space="Shared")
    tern_o = nc.dram_tensor("tern_o", [4, 64, 128, 512], BF16, addr_space="Shared")
    tzg = nc.dram_tensor("tzg", [2, 128, FC, 512], BF16)
    tzv = nc.dram_tensor("tzv", [2, 128, FC, 512], BF16)
    tzo = nc.dram_tensor("tzo", [32, 128, 512], BF16)
    grin = nc.dram_tensor("grin", [1, 128], F32)
    grout = nc.dram_tensor("grout", [1, 128], F32)
    qts = nc.dram_tensor("qts", [NH, 128, 1024], BF16)
    obuf_d = nc.dram_tensor("obuf_d", [NH, 128, 1024], BF16)
    agin = nc.dram_tensor("agin", [NH, 128, 128], F32)
    agout = nc.dram_tensor("agout", [4, NH, 128, 128], F32)

    R = nc.alloc_sbuf_tensor("R", [128, NT * D], F32)
    identb = nc.alloc_sbuf_tensor("identb", [128, 128], BF16)
    identf = nc.alloc_sbuf_tensor("identf", [128, 128], F32)
    onesb = nc.alloc_sbuf_tensor("onesb", [128, 1], BF16)
    onesf = nc.alloc_sbuf_tensor("onesf", [128, 1], F32)
    onesrb = nc.alloc_sbuf_tensor("onesrb", [1, 128], BF16)
    onesrf = nc.alloc_sbuf_tensor("onesrf", [1, 128], F32)
    epsln = nc.alloc_sbuf_tensor("epsln", [128, 1], F32)
    magic = nc.alloc_sbuf_tensor("magic", [128, 1], F32)
    normw_s = nc.alloc_sbuf_tensor("normw_s", [128, 16], F32)
    slopeb = nc.alloc_sbuf_tensor("slopeb", [128, 16], F32)
    nslopeb = nc.alloc_sbuf_tensor("nslopeb", [128, 16], F32)
    stat = nc.alloc_sbuf_tensor("stat", [128, 32], F32)
    gamv = nc.alloc_sbuf_tensor("gamv", [128, 8], F32)
    gb_s = nc.alloc_sbuf_tensor("gb_s", [128, 64], F32)
    vb_s = nc.alloc_sbuf_tensor("vb_s", [128, 64], F32)
    ln_sq = nc.alloc_sbuf_tensor("ln_sq", [128, D], BF16)

    S = {k: Sem(nc, k) for k in ("dS", "dG", "dT", "dV", "dW", "dX", "dY",
                                 "mm", "ev", "sc", "gp", "aux", "out", "cc")}
    TERN_CC = [0]

    p = Ph(nc)
    p.o("sync", lambda e: e.dma_start(
        out=R.ap().rearrange("p (t f) -> p t f", t=NT),
        in_=x_own.ap().rearrange("(t p) f -> p t f", p=128)), i=[(S["dS"], 16)])

    p.o("sync", lambda e: e.dma_start(
        out=normw_s[:, :], in_=normw_in.ap().rearrange("(h p) -> p h", p=128),
        allow_slow_non_contiguous=True),
        i=[(S["dS"], 16)])
    p.o("sync", lambda e: e.dma_start(out=slopeb[:, :], in_=slope_in[:, :]),
        i=[(S["dS"], 16)])
    p.o("sync", lambda e: e.dma_start(
        out=gb_s[:, :], in_=gb_d.ap().rearrange("(c p) -> p c", p=128),
        allow_slow_non_contiguous=True), i=[(S["dS"], 16)])
    p.o("sync", lambda e: e.dma_start(
        out=vb_s[:, :], in_=vb_d.ap().rearrange("(c p) -> p c", p=128),
        allow_slow_non_contiguous=True), i=[(S["dS"], 16)])

    dma_n = S["dS"].n
    p.o("gpsimd", lambda g: g.memset(identb[:, :], 0.0))
    p.o("gpsimd", lambda g: g.memset(identf[:, :], 0.0))
    p.o("gpsimd", lambda g: g.memset(onesb[:, :], 1.0))
    p.o("gpsimd", lambda g: g.memset(onesf[:, :], 1.0))
    p.o("gpsimd", lambda g: g.memset(epsln[:, :], EPS_LN))
    p.o("gpsimd", lambda g: g.memset(onesrb[:, :], 1.0))
    p.o("gpsimd", lambda g: g.memset(onesrf[:, :], 1.0))
    p.o("gpsimd", lambda g: g.memset(magic[:, :], MAGIC))
    p.o("gpsimd", lambda g: g.drain())
    p.o("gpsimd", lambda g: g.affine_select(
        out=identb[:, :], in_=identb[:, :], compare_op=ALU.not_equal, fill=1.0,
        base=0, pattern=[[-1, 128]], channel_multiplier=1))
    p.o("gpsimd", lambda g: g.affine_select(
        out=identf[:, :], in_=identf[:, :], compare_op=ALU.not_equal, fill=1.0,
        base=0, pattern=[[-1, 128]], channel_multiplier=1))
    p.o("vector", lambda v: v.tensor_scalar_mul(nslopeb[:, :], slopeb[:, :], -1.0),
        w=[(S["dS"], dma_n)], i=[(S["ev"], 1)])
    p.o("gpsimd", lambda g: g.drain(), w=[(S["dS"], dma_n)], i=[(S["gp"], 1)])
    p.close()

    def ln_tile(p, src_ap, dst_ap, w0=()):
        p.o("vector", lambda v: v.reduce_sum(stat[:, 0:1], src_ap, axis=AXX),
            w=w0, i=[(S["ev"], 1)])
        p.o("scalar", lambda s: s.activation(ln_sq[:, :], src_ap, AF.Square,
                                             accum_out=stat[:, 1:2]),
            w=w0, i=[(S["sc"], 1)])
        p.o("vector", lambda v: v.tensor_scalar_mul(stat[:, 2:3], stat[:, 0:1], 1.0 / D),
            w=[S["ev"].cur()], i=[(S["ev"], 1)])
        ev_mean = S["ev"].n
        p.o("scalar", lambda s: s.activation(stat[:, 3:4], stat[:, 2:3], AF.Square),
            w=[(S["ev"], ev_mean)], i=[(S["sc"], 1)])
        sc_msq = S["sc"].n
        p.o("vector", lambda v: v.scalar_tensor_tensor(
            stat[:, 3:4], stat[:, 1:2], 1.0 / D, stat[:, 3:4], ALU.mult, ALU.subtract),
            w=[(S["sc"], sc_msq)], i=[(S["ev"], 1)])
        ev_var = S["ev"].n
        p.o("scalar", lambda s: s.activation(stat[:, 3:4], stat[:, 3:4], AF.Ln,
                                             bias=epsln[:, 0:1]),
            w=[(S["ev"], ev_var)], i=[(S["sc"], 1)])
        p.o("scalar", lambda s: s.activation(stat[:, 4:5], stat[:, 3:4], AF.Exp, scale=-0.5),
            w=[S["sc"].cur()], i=[(S["sc"], 1)])
        sc_rstd = S["sc"].n
        p.o("vector", lambda v: v.tensor_scalar(
            dst_ap, src_ap, stat[:, 2:3], stat[:, 4:5], ALU.subtract, ALU.mult),
            w=[(S["sc"], sc_rstd), (S["mm"], S["mm"].n)], i=[(S["ev"], 1)])
        return S["ev"].n

    def ln_stats(p, srcs, w0s=None):
        T = len(srcs)
        sq_gate = S["sc"].cur()
        for t, src in enumerate(srcs):
            w0 = tuple(w0s[t]) if w0s else ()
            p.o("vector", lambda v, t=t, src=src: v.reduce_sum(
                stat[:, 8 + t:9 + t], src, axis=AXX), w=w0, i=[(S["ev"], 1)])
            p.o("scalar", lambda s, t=t, src=src: s.activation(
                ln_sq[:, :], src, AF.Square, accum_out=stat[:, 16 + t:17 + t]),
                w=w0 + (sq_gate,), i=[(S["sc"], 1)])
            sq_gate = S["sc"].cur()
        ev_t, sc_t = S["ev"].n, S["sc"].n
        p.o("vector", lambda v, T=T: v.tensor_scalar_mul(
            stat[:, 0:T], stat[:, 8:8 + T], 1.0 / D),
            w=[(S["ev"], ev_t), (S["sc"], sc_t)], i=[(S["ev"], 1)])
        ev_t = S["ev"].n
        p.o("scalar", lambda s, T=T: s.activation(
            stat[:, 24:24 + T], stat[:, 0:T], AF.Square),
            w=[(S["ev"], ev_t)], i=[(S["sc"], 1)])
        sc_t = S["sc"].n
        p.o("vector", lambda v, T=T: v.scalar_tensor_tensor(
            stat[:, 24:24 + T], stat[:, 16:16 + T], 1.0 / D, stat[:, 24:24 + T],
            ALU.mult, ALU.subtract),
            w=[(S["sc"], sc_t)], i=[(S["ev"], 1)])
        ev_t = S["ev"].n
        p.o("scalar", lambda s, T=T: s.activation(
            stat[:, 24:24 + T], stat[:, 24:24 + T], AF.Ln, bias=epsln[:, 0:1]),
            w=[(S["ev"], ev_t)], i=[(S["sc"], 1)])
        p.o("scalar", lambda s, T=T: s.activation(
            stat[:, 24:24 + T], stat[:, 24:24 + T], AF.Exp, scale=-0.5),
            w=[S["sc"].cur()], i=[(S["sc"], 1)])
        return (S["sc"], S["sc"].n)

    def ln_xt_build(p, srcs, dstT, stride, psb, xl2, w0s=None, t0=0, tfree0=None):
        tfree = {0: tfree0 or (S["mm"], 0), 1: tfree0 or (S["mm"], 0)}
        nguard = {0: (S["sc"], 0), 1: (S["sc"], 0)}
        for t, src in enumerate(srcs):
            b = t % 2
            w0 = tuple(w0s[t]) if w0s else ()
            for cch in range(4):
                p.o("vector", lambda v, src=src, cch=cch: v.bn_stats(
                    stat[:, 8 + cch * 6:14 + cch * 6],
                    src[:, cch * 512:(cch + 1) * 512]),
                    w=(w0 if cch == 0 else ()), i=[(S["ev"], 1)])
            p.o("vector", lambda v, b=b: v.bn_aggr(
                stat[:, b * 4:b * 4 + 2], stat[:, 8:32]),
                w=[S["ev"].cur(), nguard[b]], i=[(S["ev"], 1)])
            ev_mv = S["ev"].n
            p.o("scalar", lambda s, b=b: s.activation(
                stat[:, b * 4 + 1:b * 4 + 2], stat[:, b * 4 + 1:b * 4 + 2],
                AF.Ln, bias=epsln[:, 0:1]), w=[(S["ev"], ev_mv)], i=[(S["sc"], 1)])
            p.o("scalar", lambda s, b=b: s.activation(
                stat[:, b * 4 + 1:b * 4 + 2], stat[:, b * 4 + 1:b * 4 + 2],
                AF.Exp, scale=-0.5), w=[S["sc"].cur()], i=[(S["sc"], 1)])
            p.o("scalar", lambda s, b=b: s.activation(
                stat[:, b * 4 + 2:b * 4 + 3], stat[:, b * 4:b * 4 + 1],
                AF.Copy, scale=-1.0), w=[S["sc"].cur()], i=[(S["sc"], 1)])
            p.o("scalar", lambda s, b=b: s.activation(
                stat[:, b * 4 + 3:b * 4 + 4], stat[:, b * 4 + 1:b * 4 + 2],
                AF.Identity, scale=stat[:, b * 4 + 2:b * 4 + 3]),
                w=[S["sc"].cur()], i=[(S["sc"], 1)])
            p.o("scalar", lambda s, src=src, b=b: s.activation(
                xl2[b][:, :], src, AF.Identity,
                scale=stat[:, b * 4 + 1:b * 4 + 2],
                bias=stat[:, b * 4 + 3:b * 4 + 4]),
                w=[S["sc"].cur(), tfree[b]], i=[(S["sc"], 1)])
            nguard[b] = (S["sc"], S["sc"].n)
            transpose_16(p, xl2[b], dstT, (t0 + t) * 128, stride, psb,
                         w0=(S["sc"], S["sc"].n))
            tfree[b] = (S["mm"], S["mm"].n)

    def transpose_16(p, src_bf16, dst_tensor, dst_base, dst_tile_stride, psb,
                     w0=None):
        for half in range(2):
            for c8 in range(8):
                c = half * 8 + c8
                p.o("tensor", lambda t, c=c, c8=c8: t.transpose(
                    psb[:, c8 * 128:(c8 + 1) * 128],
                    src_bf16[:, c * 128:(c + 1) * 128], identb[:, :]),
                    w=([S["ev"].cur()] + ([w0] if w0 else [])) if c8 == 0 else (),
                    i=[(S["mm"], 1)] if c8 == 7 else ())
            mm_t = S["mm"].n
            dst = bass.AP(dst_tensor.ap().tensor,
                          (half * 8) * dst_tile_stride + dst_base,
                          [[dst_tensor.ap().ap[0][0], 128], [dst_tile_stride, 8], [1, 128]])
            srcp = bass.AP(psb.ap().tensor, 0,
                           [[psb.ap().ap[0][0], 128], [128, 8], [1, 128]])
            p.o("vector", lambda v, dst=dst, srcp=srcp: v.tensor_copy(dst, srcp),
                w=[(S["mm"], mm_t)], i=[(S["ev"], 1)])

    def attn_a():
        with contextlib.ExitStack() as st:
            xT = st.enter_context(nc.sbuf_tensor("aa_xT", [128, FC * 1024], BF16))
            xpst = st.enter_context(nc.sbuf_tensor("aa_xp", [128, D], F32))
            xlnb = st.enter_context(nc.sbuf_tensor("aa_xlnb", [128, D], BF16))
            xlnb_b = st.enter_context(nc.sbuf_tensor("aa_xlnbb", [128, D], BF16))
            qT = st.enter_context(nc.sbuf_tensor("aa_qT", [128, NH * 512], BF16))
            kT = st.enter_context(nc.sbuf_tensor("aa_kT", [128, 4 * 1024], BF16))
            vT = st.enter_context(nc.sbuf_tensor("aa_vT", [128, 4 * 1024], BF16))
            maskS = st.enter_context(nc.sbuf_tensor("aa_maskS", [128, 4 * 1024], BF16))
            pmrow = st.enter_context(nc.sbuf_tensor("aa_pmrow", [1, 512], BF16))
            p_sb = st.enter_context(nc.sbuf_tensor("aa_p", [128, 4 * 1024], BF16))
            pT = st.enter_context(nc.sbuf_tensor("aa_pT", [128, 8 * 512], BF16))
            oT = st.enter_context(nc.sbuf_tensor("aa_oT", [128, NH * 512], BF16))
            wbuf = st.enter_context(nc.sbuf_tensor("aa_w", [128, FC * 512], BF16))
            ps = st.enter_context(nc.psum_tensor("aa_ps", [128, 2560], F32))
            psb = st.enter_context(nc.psum_tensor("aa_psb", [128, 2048], BF16))

            p = Ph(nc)
            p.o("gpsimd", lambda g: g.dma_start(out=pmrow[0:1, :], in_=pmrow_in[0:1, :]),
                i=[(S["dG"], 16)])
            dma_pm = S["dG"].n
            for qt in range(4):
                p.o("gpsimd", lambda g, qt=qt: g.memset(
                    maskS[:, qt * 1024:(qt + 1) * 1024], 0.0))
            p.o("gpsimd", lambda g: g.drain())
            for qt in range(4):
                mprev = maskS[:, qt * 1024: qt * 1024 + 512]
                mcur = maskS[:, qt * 1024 + 512: qt * 1024 + 1024]
                p.o("gpsimd", lambda g, mprev=mprev, qt=qt: g.affine_select(
                    out=mprev, in_=mprev, compare_op=ALU.is_ge, fill=NEG,
                    base=-(qt * 128) - 1, channel_multiplier=-1, pattern=[[1, 512]]))
                p.o("gpsimd", lambda g, mcur=mcur, qt=qt: g.affine_select(
                    out=mcur, in_=mcur, compare_op=ALU.is_ge, fill=NEG,
                    base=qt * 128, channel_multiplier=1, pattern=[[-1, 512]]))
            p.o("gpsimd", lambda g: g.drain(), w=[(S["dG"], dma_pm)], i=[(S["gp"], 1)])
            gp_mask = S["gp"].n
            p.close()

            def ph_xt(blk):
                p = Ph(nc)
                if blk == 0:
                    for wt in range(4):
                        dma_t = S["dS"].n + 16
                        p.o("sync", lambda e, wt=wt: e.dma_start(
                            out=xpst[:, :],
                            in_=x_prev.ap().rearrange("(t p) f -> p t f", p=128)[:, wt, :]),
                            w=[S["aux"].cur()], i=[(S["dS"], 16)])
                        ev_dst = ln_tile(p, xpst[:, :], xlnb[:, :],
                                         w0=((S["dS"], dma_t),))
                        transpose_16(p, xlnb, xT, wt * 128, 1024, psb)
                        p.o("vector", lambda v: v.engine_nop(), w=[(S["ev"], ev_dst)],
                            i=[(S["aux"], 1)])
                    srcs = [R[:, t * D:(t + 1) * D] for t in range(4)]
                    ln_xt_build(p, srcs, xT, 1024, psb, [xlnb, xlnb_b], t0=4,
                                tfree0=(S["mm"], S["mm"].n))
                else:
                    own = [0, 1, 2, 3, 4, 5, 6, 7]
                    srcs = [R[:, t * D:(t + 1) * D] for t in own]
                    ln_xt_build(p, srcs, xT, 1024, psb, [xlnb, xlnb_b], t0=0)
                p.close()

            def ph_qkv(blk):
                p = Ph(nc)
                bank_free = {b: (S["ev"], 0) for b in range(4)}
                wbuf_free = (S["mm"], 0)
                for j in range(4):
                    dma_t = S["dG"].n + 16
                    p.o("gpsimd", lambda e, j=j: e.dma_start(
                        out=wbuf.ap().rearrange("p (c o) -> p c o", c=FC),
                        in_=wq_d.ap()[j]),
                        w=[wbuf_free], i=[(S["dG"], 16)])
                    for oc in range(4):
                        h = j * 4 + oc
                        for c in range(FC):
                            p.o("tensor", lambda t, c=c, oc=oc: t.matmul(
                                ps[:, oc * 512:(oc + 1) * 512],
                                wbuf[:, c * 512 + oc * 128: c * 512 + oc * 128 + 128],
                                xT[:, c * 1024 + 512:(c + 1) * 1024],
                                start=(c == 0), stop=(c == FC - 1)),
                                w=[(S["dG"], dma_t), bank_free[oc]] if c == 0 else (),
                                i=[(S["mm"], 1)] if c == FC - 1 else ())
                        mm_t = S["mm"].n
                        p.o("vector", lambda v, oc=oc, h=h: v.tensor_scalar_mul(
                            qT[:, h * 512:(h + 1) * 512], ps[:, oc * 512:(oc + 1) * 512],
                            0.08838834764831845),
                            w=[(S["mm"], mm_t)], i=[(S["ev"], 1)])
                        bank_free[oc] = (S["ev"], S["ev"].n)
                    wbuf_free = (S["mm"], S["mm"].n)
                for wt_d, dstT in ((wk_d, kT), (wv_d, vT)):
                    dma_t = S["dG"].n + 16
                    p.o("gpsimd", lambda e, wt_d=wt_d: e.dma_start(
                        out=wbuf.ap().rearrange("p (c o) -> p c o", c=FC)[:, :, 0:512],
                        in_=wt_d.ap()[0]),
                        w=[wbuf_free], i=[(S["dG"], 16)])
                    for oc in range(4):
                        for mh in range(2):
                            for c in range(FC):
                                p.o("tensor", lambda t, c=c, oc=oc, mh=mh: t.matmul(
                                    ps[:, oc * 512:(oc + 1) * 512],
                                    wbuf[:, c * 512 + oc * 128: c * 512 + oc * 128 + 128],
                                    xT[:, c * 1024 + mh * 512: c * 1024 + (mh + 1) * 512],
                                    start=(c == 0), stop=(c == FC - 1)),
                                    w=[(S["dG"], dma_t), bank_free[oc]] if c == 0 else (),
                                    i=[(S["mm"], 1)] if c == FC - 1 else ())
                            mm_t = S["mm"].n
                            p.o("vector", lambda v, oc=oc, mh=mh, dstT=dstT: v.tensor_copy(
                                dstT[:, oc * 1024 + mh * 512: oc * 1024 + (mh + 1) * 512],
                                ps[:, oc * 512:(oc + 1) * 512]),
                                w=[(S["mm"], mm_t)], i=[(S["ev"], 1)])
                            bank_free[oc] = (S["ev"], S["ev"].n)
                    wbuf_free = (S["mm"], S["mm"].n)
                p.close()

            def ph_vtr(blk):
                p = Ph(nc)
                for hk in range(4):
                    for kc in range(8):
                        p.o("tensor", lambda t, hk=hk, kc=kc: t.transpose(
                            psb[:, kc * 128:(kc + 1) * 128],
                            vT[:, hk * 1024 + kc * 128: hk * 1024 + (kc + 1) * 128],
                            identb[:, :]),
                            w=[S["ev"].cur()] if kc == 0 else (),
                            i=[(S["mm"], 1)] if kc == 7 else ())
                    mm_t = S["mm"].n
                    p.o("vector", lambda v, hk=hk: v.tensor_copy(
                        vT[:, hk * 1024:(hk + 1) * 1024], psb[:, 0:1024]),
                        w=[(S["mm"], mm_t)], i=[(S["ev"], 1)])
                p.close()

            def ph_attn(blk):
                p = Ph(nc)
                bank_free = {b: (S["sc"], 0) for b in range(2)}
                for h in range(NH):
                    hk = h // 4
                    for qt in range(4):
                        bank = qt % 2
                        nmm = 5 if blk == 0 else 4
                        for half in range(2):
                            p.o("tensor", lambda t, h=h, hk=hk, qt=qt, half=half, bank=bank:
                                t.matmul(
                                    ps[:, bank * 1024 + half * 512:
                                       bank * 1024 + (half + 1) * 512],
                                    qT[:, h * 512 + qt * 128: h * 512 + (qt + 1) * 128],
                                    kT[:, hk * 1024 + half * 512: hk * 1024 + (half + 1) * 512],
                                    start=True, stop=False),
                                w=[bank_free[bank], (S["gp"], gp_mask),
                                   S["sc"].cur()] if half == 0 else ())
                            last = (half == 1) or (blk != 0)
                            p.o("tensor", lambda t, qt=qt, half=half, bank=bank:
                                t.matmul(
                                    ps[:, bank * 1024 + half * 512:
                                       bank * 1024 + (half + 1) * 512],
                                    identb[:, :],
                                    maskS[:, qt * 1024 + half * 512:
                                          qt * 1024 + (half + 1) * 512],
                                    start=False, stop=(half == 1 or blk != 0)),
                                i=[(S["mm"], 1)] if (half == 1) else ())
                            if half == 0 and blk == 0:
                                p.o("tensor", lambda t, bank=bank: t.matmul(
                                    ps[:, bank * 1024: bank * 1024 + 512],
                                    onesrb[0:1, :], pmrow[0:1, :],
                                    start=False, stop=True))
                        mm_t = S["mm"].n
                        p.o("vector", lambda v, qt=qt, bank=bank: v.reduce_max(
                            stat[:, 8 + qt:9 + qt], ps[:, bank * 1024:(bank + 1) * 1024],
                            axis=AXX, negate=True),
                            w=[(S["mm"], mm_t)], i=[(S["ev"], 1)])
                        ev_t = S["ev"].n
                        p.o("scalar", lambda s, qt=qt, bank=bank: s.activation(
                            p_sb[:, qt * 1024:(qt + 1) * 1024],
                            ps[:, bank * 1024:(bank + 1) * 1024],
                            AF.Exp, bias=stat[:, 8 + qt:9 + qt],
                            accum_out=stat[:, 12 + qt:13 + qt]),
                            w=[(S["ev"], ev_t), (S["mm"], S["mm"].n)], i=[(S["sc"], 1)])
                        bank_free[bank] = (S["sc"], S["sc"].n)
                        sc_t = S["sc"].n
                        p.o("vector", lambda v, qt=qt: v.reciprocal(
                            stat[:, 16 + qt:17 + qt], stat[:, 12 + qt:13 + qt]),
                            w=[(S["sc"], sc_t)], i=[(S["ev"], 1)])
                        ev_t = S["ev"].n
                        p.o("vector", lambda v, qt=qt: v.tensor_scalar_mul(
                            p_sb[:, qt * 1024:(qt + 1) * 1024],
                            p_sb[:, qt * 1024:(qt + 1) * 1024],
                            stat[:, 16 + qt:17 + qt]),
                            w=[(S["ev"], ev_t)], i=[(S["ev"], 1)])
                    ev_pnorm = S["ev"].n
                    for kc in range(8):
                        pb = kc % 2
                        for qt in range(4):
                            p.o("tensor", lambda t, kc=kc, qt=qt, pb=pb: t.transpose(
                                psb[:, pb * 1024 + qt * 128: pb * 1024 + (qt + 1) * 128],
                                p_sb[:, qt * 1024 + kc * 128: qt * 1024 + (kc + 1) * 128],
                                identb[:, :]),
                                w=[(S["ev"], ev_pnorm), S["ev"].cur()] if qt == 0 else (),
                                i=[(S["mm"], 1)] if qt == 3 else ())
                        mm_t = S["mm"].n
                        p.o("vector", lambda v, kc=kc, pb=pb: v.tensor_copy(
                            pT[:, kc * 512:(kc + 1) * 512],
                            psb[:, pb * 1024: pb * 1024 + 512]),
                            w=[(S["mm"], mm_t)], i=[(S["ev"], 1)])
                    ev_pT = S["ev"].n
                    for kc in range(8):
                        p.o("tensor", lambda t, hk=hk, kc=kc: t.matmul(
                            ps[:, 2048:2560],
                            vT[:, hk * 1024 + kc * 128: hk * 1024 + (kc + 1) * 128],
                            pT[:, kc * 512:(kc + 1) * 512],
                            start=(kc == 0), stop=(kc == 7)),
                            w=[(S["ev"], ev_pT), S["sc"].cur()] if kc == 0 else (),
                            i=[(S["mm"], 1)] if kc == 7 else ())
                    mm_av = S["mm"].n
                    p.o("scalar", lambda s, h=h: s.activation(
                        oT[:, h * 512:(h + 1) * 512], ps[:, 2048:2560], AF.Copy),
                        w=[(S["mm"], mm_av)], i=[(S["sc"], 1)])
                p.close()

            def ph_wo(blk):
                p = Ph(nc)
                bank_free = {b: (S["ev"], 0) for b in range(4)}
                wbuf_free = (S["mm"], 0)
                for fb in range(4):
                    dma_t = S["dG"].n + 16
                    p.o("gpsimd", lambda e, fb=fb: e.dma_start(
                        out=wbuf.ap().rearrange("p (c o) -> p c o", c=FC),
                        in_=wo_d.ap()[fb]),
                        w=[wbuf_free], i=[(S["dG"], 16)])
                    for qt in range(4):
                        t_own = blk * 4 + qt
                        bank = qt
                        for h in range(NH):
                            p.o("tensor", lambda t, h=h, qt=qt, bank=bank: t.matmul(
                                ps[:, bank * 512:(bank + 1) * 512],
                                oT[:, h * 512 + qt * 128: h * 512 + (qt + 1) * 128],
                                wbuf[:, h * 512: h * 512 + 512],
                                start=(h == 0), stop=(h == NH - 1)),
                                w=[(S["dG"], dma_t), bank_free[bank],
                                   S["sc"].cur() if h == 0 and qt == 0 and fb == 0
                                   else (S["sc"], 0)] if h == 0 else (),
                                i=[(S["mm"], 1)] if h == NH - 1 else ())
                        mm_t = S["mm"].n
                        p.o("vector", lambda v, t_own=t_own, fb=fb, bank=bank: v.tensor_tensor(
                            R[:, t_own * D + fb * 512: t_own * D + (fb + 1) * 512],
                            R[:, t_own * D + fb * 512: t_own * D + (fb + 1) * 512],
                            ps[:, bank * 512:(bank + 1) * 512], ALU.add),
                            w=[(S["mm"], mm_t)], i=[(S["ev"], 1)])
                        bank_free[bank] = (S["ev"], S["ev"].n)
                    wbuf_free = (S["mm"], S["mm"].n)
                p.close()

            ph_xt(0)
            ph_qkv(0)
            ph_vtr(0)
            ph_attn(0)
            ph_xt(1)
            ph_qkv(1)
            ph_vtr(1)
            ph_wo(0)
            ph_attn(1)
            ph_wo(1)
    def ffn_ternarize():
        with contextlib.ExitStack() as st:
            wst = st.enter_context(nc.sbuf_tensor("tz_w", [128, 2 * 4096], F32))
            acc = st.enter_context(nc.sbuf_tensor("tz_acc", [128, 16], F32))
            u_s = st.enter_context(nc.sbuf_tensor("tz_u", [128, 2 * 4096], F32))
            r_s = st.enter_context(nc.sbuf_tensor("tz_r", [128, 2 * 4096], F32))
            tb_s = st.enter_context(nc.sbuf_tensor("tz_t", [128, 2 * 4096], BF16))
            ps = st.enter_context(nc.psum_tensor("tz_ps", [128, 512], F32))

            CW = 4096
            NCH = 4
            NEL = 16 * 128 * FC * 512
            RG8 = [[0, 1, 2, 3, 4, 5, 6, 7]]

            def _flat(t):
                a = t.ap()
                s = "a b c d -> (a b c d)" if len(a.shape) == 4 else "a b c -> (a b c)"
                return a.rearrange(s).rearrange("(p x) -> p x", p=128)

            specs = [(_flat(wg_sh), _flat(tzg), 0),
                     (_flat(wv_sh), _flat(tzv), 1),
                     (_flat(wo_sh), _flat(tzo), 2)]

            p = Ph(nc)
            buf_free = {0: (S["sc"], 0), 1: (S["sc"], 0)}
            ci_all = 0
            for wflat, _tflat, gi in specs:
                for ci in range(NCH):
                    buf = ci_all % 2
                    bo = buf * 4096
                    dsem = S["dS"] if buf == 0 else S["dT"]
                    dma_t = dsem.n + 16
                    p.o("sync", lambda e, ci=ci, bo=bo, wflat=wflat: e.dma_start(
                        out=wst[:, bo: bo + CW],
                        in_=wflat[:, ci * CW:(ci + 1) * CW]),
                        w=[buf_free[buf]], i=[(dsem, 16)])
                    p.o("scalar", lambda s, bo=bo, gi=gi, ci=ci: s.activation(
                        u_s[:, bo: bo + CW], wst[:, bo: bo + CW], AF.Abs,
                        accum_out=acc[:, gi * 4 + ci:gi * 4 + ci + 1]),
                        w=[(dsem, dma_t)], i=[(S["sc"], 1)])
                    buf_free[buf] = (S["sc"], S["sc"].n)
                    ci_all += 1
            sc_all = S["sc"].n
            for gi in range(3):
                p.o("vector", lambda v, gi=gi: v.reduce_sum(
                    stat[:, gi:gi + 1], acc[:, gi * 4:gi * 4 + 4], axis=AXX),
                    w=[(S["sc"], sc_all)] if gi == 0 else (), i=[(S["ev"], 1)])
            ev_t = S["ev"].n
            p.o("tensor", lambda t: t.matmul(ps[0:1, 0:3], onesf[:, 0:1], stat[:, 0:3],
                                             start=True, stop=True),
                w=[(S["ev"], ev_t)], i=[(S["mm"], 1)])
            mm_t = S["mm"].n
            p.o("gpsimd", lambda g: g.memset(r_s[0:1, 0:128], 0.0), i=[(S["gp"], 1)])
            gp_t = S["gp"].n
            p.o("vector", lambda v: v.tensor_copy(r_s[0:1, 0:3], ps[0:1, 0:3]),
                w=[(S["mm"], mm_t), (S["gp"], gp_t)], i=[(S["ev"], 1)])
            ev_t = S["ev"].n
            p.o("sync", lambda e: e.dma_start(out=grin[0:1, :], in_=r_s[0:1, 0:128]),
                w=[(S["ev"], ev_t)], i=[(S["dS"], 16)])
            dma_t = S["dS"].n
            p.o("gpsimd", lambda g: g.collective_compute(
                "AllReduce", ALU.add, replica_groups=RG8,
                ins=[grin.ap().opt()], outs=[grout.ap().opt()]),
                w=[(S["dS"], dma_t)], i=[(S["cc"], 1)])
            cc_ar = S["cc"].n
            p.o("sync", lambda e: e.dma_start(out=u_s[0:1, 128:256], in_=grout[0:1, :]),
                w=[(S["cc"], cc_ar)], i=[(S["dS"], 16)])
            dma_t = S["dS"].n
            p.o("vector", lambda v: v.tensor_scalar(
                stat[0:1, 12:15], u_s[0:1, 128:131], 1.0 / NEL, 1e-5,
                ALU.mult, ALU.max),
                w=[(S["dS"], dma_t), S["ev"].cur()], i=[(S["ev"], 1)])
            p.o("vector", lambda v: v.reciprocal(stat[0:1, 15:18], stat[0:1, 12:15]),
                w=[S["ev"].cur()], i=[(S["ev"], 1)])
            ev_t = S["ev"].n
            p.o("tensor", lambda t: t.matmul(
                ps[0:128, 8:14], onesrf[0:1, :], stat[0:1, 12:18],
                start=True, stop=True),
                w=[(S["ev"], ev_t)], i=[(S["mm"], 1)])
            mm_t = S["mm"].n
            p.o("vector", lambda v: v.tensor_copy(gamv[:, 3:6], ps[:, 8:11]),
                w=[(S["mm"], mm_t)], i=[(S["ev"], 1)])
            p.o("vector", lambda v: v.tensor_copy(gamv[:, 0:3], ps[:, 11:14]),
                i=[(S["ev"], 1)])
            ev_gam = S["ev"].n
            p.close()

            p = Ph(nc)
            wfree = {0: (S["sc"], 0), 1: (S["sc"], 0)}
            ufree = {0: (S["ev"], 0), 1: (S["ev"], 0)}
            sfree = {0: (S["ev"], 0), 1: (S["ev"], 0)}
            tfree = {0: (S["dW"], 0), 1: (S["dV"], 0)}
            tsem = {0: S["dW"], 1: S["dV"]}
            ci_all = 0
            for wflat, tflat, gi in specs:
                for ci in range(NCH):
                    buf = ci_all % 2
                    bo = buf * 4096
                    dsem = S["dS"] if buf == 0 else S["dT"]
                    dma_t = dsem.n + 16
                    p.o("sync", lambda e, ci=ci, bo=bo, wflat=wflat: e.dma_start(
                        out=wst[:, bo: bo + CW],
                        in_=wflat[:, ci * CW:(ci + 1) * CW]),
                        w=[wfree[buf]], i=[(dsem, 16)])
                    p.o("scalar", lambda s, bo=bo, gi=gi: s.activation(
                        u_s[:, bo: bo + CW], wst[:, bo: bo + CW], AF.Abs,
                        scale=gamv[:, gi:gi + 1]),
                        w=([(S["ev"], ev_gam)] if ci_all == 0 else [])
                        + [(dsem, dma_t), ufree[buf]], i=[(S["sc"], 1)])
                    p.o("scalar", lambda s, bo=bo: s.activation(
                        r_s[:, bo: bo + CW], wst[:, bo: bo + CW], AF.Sign),
                        w=[sfree[buf]], i=[(S["sc"], 1)])
                    wfree[buf] = (S["sc"], S["sc"].n)
                    sc_t = S["sc"].n
                    p.o("vector", lambda v, bo=bo: v.tensor_scalar(
                        u_s[:, bo: bo + CW], u_s[:, bo: bo + CW], 0.5, None, ALU.is_ge),
                        w=[(S["sc"], sc_t)], i=[(S["ev"], 1)])
                    p.o("vector", lambda v, bo=bo: v.tensor_tensor(
                        tb_s[:, bo: bo + CW], u_s[:, bo: bo + CW], r_s[:, bo: bo + CW],
                        ALU.mult),
                        w=[S["ev"].cur(), tfree[buf]], i=[(S["ev"], 1)])
                    ufree[buf] = (S["ev"], S["ev"].n)
                    sfree[buf] = (S["ev"], S["ev"].n)
                    ev_t = S["ev"].n
                    p.o("gpsimd", lambda g, ci=ci, bo=bo, tflat=tflat: g.dma_start(
                        out=tflat[:, ci * CW:(ci + 1) * CW], in_=tb_s[:, bo: bo + CW]),
                        w=[(S["ev"], ev_t)], i=[(tsem[buf], 16)])
                    tfree[buf] = (tsem[buf], tsem[buf].n)
                    ci_all += 1
            dW_f, dV_f = S["dW"].n, S["dV"].n
            for k, (tin, tout) in enumerate(
                    ((tzg, tern_g), (tzv, tern_v), (tzo, tern_o))):
                p.o("gpsimd", lambda g, tin=tin, tout=tout: g.collective_compute(
                    "AllGather", ALU.bypass, replica_groups=RG8,
                    ins=[tin.ap().opt()], outs=[tout.ap().opt()]),
                    w=[(S["dW"], dW_f), (S["dV"], dV_f)] if k == 0 else (),
                    i=[(S["cc"], 1)])
            TERN_CC[0] = S["cc"].n
            p.close()

    def ffn(call):
        sfx = str(call)
        with contextlib.ExitStack() as st:
            xln = st.enter_context(nc.sbuf_tensor("ff_xln" + sfx, [128, D], F32))
            xdq = st.enter_context(nc.sbuf_tensor("ff_xdq" + sfx, [128, D], BF16))
            xdqT = st.enter_context(nc.sbuf_tensor("ff_xdqT" + sfx, [128, FC * 512], BF16))
            gv = st.enter_context(nc.sbuf_tensor("ff_gv" + sfx, [128, 64 * 512], BF16))
            wb = st.enter_context(nc.sbuf_tensor("ff_wb" + sfx, [128, 2 * FC * 512], BF16))
            gt_s = st.enter_context(nc.sbuf_tensor("ff_g" + sfx, [128, 512], BF16))
            vt_s = st.enter_context(nc.sbuf_tensor("ff_v" + sfx, [128, 512], BF16))
            amax = st.enter_context(nc.sbuf_tensor("ff_amax" + sfx, [128, 512], BF16))
            amlo = st.enter_context(nc.sbuf_tensor("ff_amlo" + sfx, [128, 512], BF16))
            s2b = st.enter_context(nc.sbuf_tensor("ff_s2b" + sfx, [128, 512], F32))
            fi16 = st.enter_context(nc.sbuf_tensor("ff_i16" + sfx, [128, 512], mybir.dt.int16))
            fi16b = st.enter_context(nc.sbuf_tensor("ff_i16b" + sfx, [128, 512], mybir.dt.int16))
            dsx = st.enter_context(nc.sbuf_tensor("ff_dsx" + sfx, [128, 8], F32))
            csc = st.enter_context(nc.sbuf_tensor("ff_csc" + sfx, [128, 4], F32))
            obb = st.enter_context(nc.sbuf_tensor("ff_obb" + sfx, [128, D], BF16))
            ps = st.enter_context(nc.psum_tensor("ff_ps" + sfx, [128, 3072], F32))
            psb = st.enter_context(nc.psum_tensor("ff_psb" + sfx, [128, 2048], BF16))

            p = Ph(nc)
            p.o("gpsimd", lambda e: e.dma_start(out=obb[:, :], in_=ob_r_in[:, :]),
                i=[(S["dG"], 16)])
            p.o("gpsimd", lambda e: e.wait_ge(S["dG"].h, S["dG"].n) or None)
            p.close()

            for grp in range(2):
                p = Ph(nc)
                srcs = [R[:, (grp * 4 + qt) * D:(grp * 4 + qt + 1) * D] for qt in range(4)]
                st_done = ln_stats(p, srcs)
                for qt in range(4):
                    t_own = grp * 4 + qt
                    p.o("vector", lambda v, qt=qt, t_own=t_own: v.tensor_scalar(
                        xln[:, :], R[:, t_own * D:(t_own + 1) * D],
                        stat[:, qt:qt + 1], stat[:, 24 + qt:25 + qt],
                        ALU.subtract, ALU.mult),
                        w=[st_done, S["ev"].cur()], i=[(S["ev"], 1)])
                    ev_ln = S["ev"].n
                    p.o("vector", lambda v: v.reduce_max(
                        stat[:, 12:13], xln[:, :], axis=AXX, apply_absolute_value=True),
                        w=[(S["ev"], ev_ln)], i=[(S["ev"], 1)])
                    p.o("vector", lambda v: v.tensor_scalar_max(stat[:, 12:13], stat[:, 12:13], 1e-5),
                        w=[S["ev"].cur()], i=[(S["ev"], 1)])
                    p.o("vector", lambda v, qt=qt: v.tensor_scalar_mul(
                        dsx[:, 4 + qt:5 + qt], stat[:, 12:13], 1.0 / 127.0),
                        w=[S["ev"].cur()], i=[(S["ev"], 1)])
                    p.o("vector", lambda v: v.reciprocal(stat[:, 13:14], stat[:, 12:13]),
                        w=[S["ev"].cur()], i=[(S["ev"], 1)])
                    p.o("vector", lambda v: v.tensor_scalar_mul(stat[:, 13:14], stat[:, 13:14], 127.0),
                        w=[S["ev"].cur()], i=[(S["ev"], 1)])
                    for z in range(4):
                        zs = slice(z * 512, (z + 1) * 512)
                        p.o("vector", lambda v, zs=zs: v.tensor_scalar_mul(
                            fi16[:, :], xln[:, zs], stat[:, 13:14]),
                            w=[S["ev"].cur(), S["gp"].cur()], i=[(S["ev"], 1)])
                        p.o("vector", lambda v, qt=qt, zs=zs: v.tensor_scalar_mul(
                            xdq[:, zs], fi16[:, :], dsx[:, 4 + qt:5 + qt]),
                            w=[S["ev"].cur(), (S["mm"], S["mm"].n)], i=[(S["ev"], 1)])
                    transpose_16(p, xdq, xdqT, qt * 128, 512, psb)
                p.o("gpsimd", lambda g: g.memset(amax[:, :], -3e38), i=[(S["gp"], 1)])
                p.o("gpsimd", lambda g: g.memset(amlo[:, :], 3e38), i=[(S["gp"], 1)])
                p.close()

                p = Ph(nc)
                bank_free = {b: (S["sc"], 0) for b in range(4)}
                half_free = {k: ((S["mm"], 0), (S["mm"], 0)) for k in ("g", "v")}
                hsem = {("g", 0): S["dS"], ("g", 1): S["dT"],
                        ("v", 0): S["dX"], ("v", 1): S["dY"]}
                gt_free = (S["gp"], 0)
                vt_free = (S["gp"], 0)
                for j in range(16):
                    dma_t = {}
                    for sk, tern in (("g", tern_g), ("v", tern_v)):
                        for hf in range(2):
                            sem = hsem[(sk, hf)]
                            dma_t[(sk, hf)] = sem.n + 16
                            boff = (0 if sk == "g" else FC * 512) + hf * 4096
                            wl_ld = [half_free[sk][hf]]
                            if call == 0 and j == 0 and sk == "g" and hf == 0:
                                wl_ld.append((S["cc"], TERN_CC[0]))
                            p.o("sync", lambda e, j=j, tern=tern, boff=boff, hf=hf:
                                e.dma_start(
                                    out=wb[:, boff: boff + 4096],
                                    in_=tern.ap()[j].rearrange("p c o -> p (c o)")
                                    [:, hf * 4096:(hf + 1) * 4096]),
                                w=wl_ld, i=[(sem, 16)])
                    lo_rec = {}
                    hi_rec = {}
                    for oc in range(4):
                        hc = j * 4 + oc
                        gbank = oc % 2
                        vbank = 2 + oc % 2
                        for sk, bank in (("g", gbank), ("v", vbank)):
                            boff = 0 if sk == "g" else FC * 512
                            for c in range(FC):
                                wl = ()
                                if c == 0:
                                    wl = [(hsem[(sk, 0)], dma_t[(sk, 0)]),
                                          bank_free[bank]]
                                    if (j, oc) == (0, 0) and sk == "g":
                                        wl.append(S["ev"].cur())
                                elif c == 8:
                                    wl = [(hsem[(sk, 1)], dma_t[(sk, 1)])]
                                p.o("tensor", lambda t, c=c, oc=oc, bank=bank, boff=boff:
                                    t.matmul(
                                        ps[:, bank * 512:(bank + 1) * 512],
                                        wb[:, boff + c * 512 + oc * 128:
                                           boff + c * 512 + oc * 128 + 128],
                                        xdqT[:, c * 512:(c + 1) * 512],
                                        start=(c == 0), stop=(c == FC - 1)),
                                    w=wl,
                                    i=[(S["mm"], 1)] if c in (7, FC - 1) else ())
                                if c == 7:
                                    lo_rec[sk] = (S["mm"], S["mm"].n)
                                elif c == FC - 1:
                                    hi_rec[sk] = (S["mm"], S["mm"].n)
                        mm_t = S["mm"].n
                        p.o("scalar", lambda s, hc=hc, gbank=gbank: s.activation(
                            gt_s[:, :], ps[:, gbank * 512:(gbank + 1) * 512], AF.Silu,
                            scale=gamv[:, 3:4], bias=gb_s[:, hc:hc + 1]),
                            w=[(S["mm"], mm_t), gt_free, S["gp"].cur()
                               if (j, oc) == (0, 0) else (S["gp"], 0)],
                            i=[(S["sc"], 1)])
                        bank_free[gbank] = (S["sc"], S["sc"].n)
                        p.o("vector", lambda v, hc=hc, vbank=vbank: v.tensor_scalar(
                            vt_s[:, :], ps[:, vbank * 512:(vbank + 1) * 512],
                            gamv[:, 4:5], vb_s[:, hc:hc + 1], ALU.mult, ALU.add),
                            w=[vt_free, (S["mm"], S["mm"].n)], i=[(S["ev"], 1)])
                        bank_free[vbank] = (S["ev"], S["ev"].n)
                        sc_t = S["sc"].n
                        ev_t = S["ev"].n
                        p.o("gpsimd", lambda g, hc=hc: g.tensor_mul(
                            gv[:, hc * 512:(hc + 1) * 512], gt_s[:, :], vt_s[:, :]),
                            w=[(S["sc"], sc_t), (S["ev"], ev_t), S["gp"].cur()],
                            i=[(S["gp"], 1)])
                        gt_free = (S["gp"], S["gp"].n)
                        vt_free = (S["gp"], S["gp"].n)
                        p.o("vector", lambda v, hc=hc: v.tensor_tensor(
                            amax[:, :], amax[:, :], gv[:, hc * 512:(hc + 1) * 512],
                            ALU.max),
                            w=[S["gp"].cur(), S["ev"].cur()], i=[(S["ev"], 1)])
                        p.o("vector", lambda v, hc=hc: v.tensor_tensor(
                            amlo[:, :], amlo[:, :], gv[:, hc * 512:(hc + 1) * 512],
                            ALU.min),
                            w=[S["ev"].cur()], i=[(S["ev"], 1)])
                    half_free = {sk: (lo_rec[sk], hi_rec[sk]) for sk in ("g", "v")}
                p.close()

                p = Ph(nc)
                p.o("vector", lambda v: v.tensor_scalar_mul(amlo[:, :], amlo[:, :], -1.0),
                    w=[S["gp"].cur(), S["ev"].cur()], i=[(S["ev"], 1)])
                p.o("vector", lambda v: v.tensor_max(amax[:, :], amax[:, :], amlo[:, :]),
                    w=[S["ev"].cur()], i=[(S["ev"], 1)])
                ev_am = S["ev"].n
                for qt in range(4):
                    p.o("tensor", lambda t, qt=qt: t.transpose(
                        psb[:, qt * 512: qt * 512 + 128],
                        amax[:, qt * 128:(qt + 1) * 128], identb[:, :]),
                        w=[(S["ev"], ev_am)] if qt == 0 else (), i=[(S["mm"], 1)])
                    mm_t = S["mm"].n
                    p.o("vector", lambda v, qt=qt: v.reduce_max(
                        stat[:, 12 + qt:13 + qt], psb[:, qt * 512: qt * 512 + 128], axis=AXX),
                        w=[(S["mm"], mm_t)], i=[(S["ev"], 1)])
                for qt in range(4):
                    p.o("vector", lambda v, qt=qt: v.tensor_scalar_max(
                        stat[:, 12 + qt:13 + qt], stat[:, 12 + qt:13 + qt], 1e-5),
                        w=[S["ev"].cur()], i=[(S["ev"], 1)])
                for qt in range(4):
                    p.o("vector", lambda v, qt=qt: v.tensor_scalar_mul(
                        dsx[:, qt:qt + 1], stat[:, 12 + qt:13 + qt], 1.0 / 127.0),
                        w=[S["ev"].cur()], i=[(S["ev"], 1)])
                    p.o("vector", lambda v, qt=qt: v.reciprocal(
                        stat[:, 16 + qt:17 + qt], stat[:, 12 + qt:13 + qt]),
                        w=[S["ev"].cur()], i=[(S["ev"], 1)])
                for qt in range(4):
                    p.o("vector", lambda v, qt=qt: v.tensor_scalar_mul(
                        stat[:, 16 + qt:17 + qt], stat[:, 16 + qt:17 + qt], 127.0),
                        w=[S["ev"].cur()], i=[(S["ev"], 1)])
                    p.o("vector", lambda v, qt=qt: v.tensor_mul(
                        csc[:, qt:qt + 1], dsx[:, qt:qt + 1], gamv[:, 5:6]),
                        w=[S["ev"].cur()], i=[(S["ev"], 1)])
                ev_s = S["ev"].n
                for qt in range(4):
                    p.o("tensor", lambda t, qt=qt: t.transpose(
                        ps[0:1, 2048 + qt * 128: 2048 + (qt + 1) * 128],
                        stat[:, 16 + qt:17 + qt], identf[:, :]),
                        w=[(S["ev"], ev_s)] if qt == 0 else (),
                        i=[(S["mm"], 1)] if qt == 3 else ())
                mm_t = S["mm"].n
                p.o("vector", lambda v: v.tensor_copy(xln[0:1, 1024:1536], ps[0:1, 2048:2560]),
                    w=[(S["mm"], mm_t)], i=[(S["ev"], 1)])
                ev_t = S["ev"].n
                p.o("tensor", lambda t: t.matmul(
                    ps[:, 2560:3072], onesrf[0:1, :], xln[0:1, 1024:1536],
                    start=True, stop=True),
                    w=[(S["ev"], ev_t)], i=[(S["mm"], 1)])
                mm_s2 = S["mm"].n
                p.o("vector", lambda v: v.tensor_copy(s2b[:, :], ps[:, 2560:3072]),
                    w=[(S["mm"], mm_s2)], i=[(S["ev"], 1)])
                ev_s2 = S["ev"].n
                p.close()

                p = Ph(nc)
                gp_q = S["gp"].n
                fi16x = {0: fi16, 1: fi16b}
                fi_free = {0: (S["sc"], 0), 1: (S["sc"], 0)}
                idc = {}
                wob_free = {b: (S["mm"], 0) for b in range(4)}
                wob_sem = {0: S["dS"], 1: S["dT"], 2: S["dX"], 3: S["dY"]}
                bank_free = {b: (S["ev"], 0) for b in range(4)}
                for fb in range(4):
                    for hp in range(32):
                        buf = hp % 4
                        dsem = wob_sem[buf]
                        dma_t = dsem.n + 16
                        wob_a = xdqT[:, buf * 1024: buf * 1024 + 1024]
                        p.o("sync", lambda g, fb=fb, hp=hp, wob_a=wob_a: g.dma_start(
                            out=wob_a.rearrange("p (h o) -> p h o", h=2),
                            in_=tern_o.ap()[fb, 2 * hp:2 * hp + 2].rearrange(
                                "h p o -> p h o")),
                            w=[wob_free[buf]], i=[(dsem, 16)])
                        if fb == 0:
                            for h2 in range(2):
                                hc = 2 * hp + h2
                                qb = hc % 2
                                p.o("vector", lambda v, hc=hc, qb=qb: v.tensor_tensor(
                                    fi16x[qb][:, :],
                                    gv[:, hc * 512:(hc + 1) * 512], s2b[:, :],
                                    ALU.mult),
                                    w=([(S["ev"], ev_s2), (S["gp"], gp_q)]
                                       if hc == 0 else []) + [fi_free[qb]],
                                    i=[(S["ev"], 1)])
                                ev_t = S["ev"].n
                                p.o("scalar", lambda s, hc=hc, qb=qb: s.activation(
                                    gv[:, hc * 512:(hc + 1) * 512], fi16x[qb][:, :],
                                    AF.Identity),
                                    w=[(S["ev"], ev_t)], i=[(S["sc"], 1)])
                                fi_free[qb] = (S["sc"], S["sc"].n)
                                idc[hc] = S["sc"].n
                        for h2 in range(2):
                            hc = 2 * hp + h2
                            for qt in range(4):
                                wl = []
                                if qt == 0 and h2 == 0 and hp == 0:
                                    wl += [bank_free[b] for b in range(4)]
                                if qt == 0 and h2 == 0:
                                    wl.append((dsem, dma_t))
                                if qt == 0 and fb == 0:
                                    wl.append((S["sc"], idc[hc]))
                                p.o("tensor", lambda t, hc=hc, qt=qt, buf=buf, h2=h2:
                                    t.matmul(
                                        ps[:, qt * 512:(qt + 1) * 512],
                                        gv[:, hc * 512 + qt * 128:
                                           hc * 512 + (qt + 1) * 128],
                                        xdqT[:, buf * 1024 + h2 * 512:
                                             buf * 1024 + (h2 + 1) * 512],
                                        start=(hc == 0), stop=(hc == 63)),
                                    w=wl, i=[(S["mm"], 1)] if (qt == 3 and h2 == 1)
                                    else ())
                        wob_free[buf] = (S["mm"], S["mm"].n)
                    mm_t = S["mm"].n
                    for qt in range(4):
                        t_own = grp * 4 + qt
                        p.o("vector", lambda v, t_own=t_own, fb=fb, qt=qt:
                            v.scalar_tensor_tensor(
                                R[:, t_own * D + fb * 512: t_own * D + (fb + 1) * 512],
                                ps[:, qt * 512:(qt + 1) * 512], csc[:, qt:qt + 1],
                                R[:, t_own * D + fb * 512: t_own * D + (fb + 1) * 512],
                                ALU.mult, ALU.add),
                            w=[(S["mm"], mm_t), S["ev"].cur()] if qt == 0 else
                            [(S["mm"], mm_t)], i=[(S["ev"], 1)])
                        ev_t = S["ev"].n
                        p.o("vector", lambda v, t_own=t_own, fb=fb: v.tensor_tensor(
                            R[:, t_own * D + fb * 512: t_own * D + (fb + 1) * 512],
                            R[:, t_own * D + fb * 512: t_own * D + (fb + 1) * 512],
                            obb[:, fb * 512:(fb + 1) * 512], ALU.add),
                            w=[(S["ev"], ev_t)], i=[(S["ev"], 1)])
                        bank_free[qt] = (S["ev"], S["ev"].n)
                p.close()

    def attn_b():
        with contextlib.ExitStack() as st:
            xT = st.enter_context(nc.sbuf_tensor("ab_xT", [128, FC * 512], BF16))
            xlnb = st.enter_context(nc.sbuf_tensor("ab_xlnb", [128, D], BF16))
            xlnb_b = st.enter_context(nc.sbuf_tensor("ab_xlnbb", [128, D], BF16))
            qT = st.enter_context(nc.sbuf_tensor("ab_qT", [128, NH * 512], BF16))
            kT = st.enter_context(nc.sbuf_tensor("ab_kT", [128, NH * 512], BF16))
            vTt = st.enter_context(nc.sbuf_tensor("ab_vT", [128, NH * 512], BF16))
            ks = st.enter_context(nc.sbuf_tensor("ab_ks", [128, 256], BF16))
            vtm = st.enter_context(nc.sbuf_tensor("ab_vtm", [128, 256], BF16))
            sdT = st.enter_context(nc.sbuf_tensor("ab_sdT", [128, 512], BF16))
            qd = st.enter_context(nc.sbuf_tensor("ab_qd", [128, 256], BF16))
            ddT = st.enter_context(nc.sbuf_tensor("ab_ddT", [128, 512], BF16))
            ddf = st.enter_context(nc.sbuf_tensor("ab_ddf", [128, 512], BF16))
            qdec = st.enter_context(nc.sbuf_tensor("ab_qdec", [128, 256], F32))
            kdT = st.enter_context(nc.sbuf_tensor("ab_kdT", [128, 32], F32))
            iot256 = st.enter_context(nc.sbuf_tensor("ab_iot", [128, 256], F32))
            kiot = st.enter_context(nc.sbuf_tensor("ab_kiot", [128, 2], F32))
            c256 = st.enter_context(nc.sbuf_tensor("ab_c256", [128, 1], F32))
            bdb = st.enter_context(nc.sbuf_tensor("ab_bdb", [128, 16], F32))
            state = st.enter_context(nc.sbuf_tensor("ab_st", [128, NH * 128], F32))
            stb = st.enter_context(nc.sbuf_tensor("ab_stb", [128, NH * 128], BF16))
            ost = st.enter_context(nc.sbuf_tensor("ab_ost", [128, NH * 256], BF16))
            wbuf = st.enter_context(nc.sbuf_tensor("ab_w", [128, 2 * FC * 512], BF16))
            ps = st.enter_context(nc.psum_tensor("ab_ps", [128, 2048], F32))
            psb = st.enter_context(nc.psum_tensor("ab_psb", [128, 1024], BF16))

            p = Ph(nc)
            p.o("gpsimd", lambda g: g.iota(iot256[:, :], [[1, 256]], base=1,
                                           channel_multiplier=0,
                                           allow_small_or_imprecise_dtypes=True),
                i=[(S["gp"], 1)])
            for ncc in range(2):
                p.o("gpsimd", lambda g, ncc=ncc: g.iota(
                    kiot[:, ncc:ncc + 1], [[1, 1]], base=255 - ncc * 128,
                    channel_multiplier=-1, allow_small_or_imprecise_dtypes=True),
                    i=[(S["gp"], 1)])
            p.o("gpsimd", lambda g: g.memset(c256[:, :], 256.0), i=[(S["gp"], 1)])
            gp_t = S["gp"].n
            for h in range(NH):
                p.o("scalar", lambda s, h=h: s.activation(
                    bdb[:, h:h + 1], c256[:, 0:1], AF.Exp, scale=nslopeb[:, h:h + 1]),
                    w=[(S["gp"], gp_t)] if h == 0 else (), i=[(S["sc"], 1)])
                for ncc in range(2):
                    p.o("scalar", lambda s, h=h, ncc=ncc: s.activation(
                        kdT[:, h * 2 + ncc:h * 2 + ncc + 1], kiot[:, ncc:ncc + 1],
                        AF.Exp, scale=nslopeb[:, h:h + 1]), i=[(S["sc"], 1)])
            for ncc in range(2):
                p.o("gpsimd", lambda g, ncc=ncc: g.iota(
                    ddf[:, ncc * 256:(ncc + 1) * 256], [[-1, 256]], base=ncc * 128,
                    channel_multiplier=1, allow_small_or_imprecise_dtypes=True),
                    i=[(S["gp"], 1)])
                p.o("gpsimd", lambda g, ncc=ncc: g.affine_select(
                    out=ddf[:, ncc * 256:(ncc + 1) * 256],
                    in_=ddf[:, ncc * 256:(ncc + 1) * 256],
                    compare_op=ALU.is_ge, fill=NEG,
                    base=-(ncc * 128), channel_multiplier=-1, pattern=[[1, 256]]),
                    w=[S["gp"].cur()], i=[(S["gp"], 1)])
            p.o("gpsimd", lambda g: g.memset(state[:, :], 0.0), i=[(S["gp"], 1)])
            p.o("gpsimd", lambda g: g.memset(stb[:, :], 0.0), i=[(S["gp"], 1)])
            p.close()

            for sb in range(2):
                p = Ph(nc)
                srcs = [R[:, (sb * 4 + wt) * D:(sb * 4 + wt + 1) * D] for wt in range(4)]
                ln_xt_build(p, srcs, xT, 512, psb, [xlnb, xlnb_b], t0=0)
                p.close()

                p = Ph(nc)
                bank_free = {b: (S["sc"], 0) for b in range(4)}
                wfree = {0: (S["dS"], 0), 1: (S["dS"], 0)}
                for j in range(12):
                    buf = j % 2
                    dsem = S["dG"] if buf == 0 else S["dV"]
                    dma_t = dsem.n + 16
                    p.o("gpsimd", lambda g, j=j, buf=buf: g.dma_start(
                        out=wbuf.ap().rearrange("p (b c o) -> p b c o", b=2, c=FC)[:, buf],
                        in_=wqkv_d.ap()[j]),
                        w=[wfree[buf]], i=[(dsem, 16)])
                    for oc in range(4):
                        o_glob = j * 512 + oc * 128
                        bank = oc
                        for c in range(FC):
                            p.o("tensor", lambda t, c=c, oc=oc, buf=buf, bank=bank: t.matmul(
                                ps[:, bank * 512:(bank + 1) * 512],
                                wbuf[:, buf * FC * 512 + c * 512 + oc * 128:
                                     buf * FC * 512 + c * 512 + oc * 128 + 128],
                                xT[:, c * 512:(c + 1) * 512],
                                start=(c == 0), stop=(c == FC - 1)),
                                w=[(dsem, dma_t), bank_free[bank],
                                   S["ev"].cur() if (j, oc) == (0, 0) else (S["ev"], 0)]
                                if c == 0 else (),
                                i=[(S["mm"], 1)] if c == FC - 1 else ())
                        mm_t = S["mm"].n
                        if o_glob < 2048:
                            dstT, off = qT, o_glob // 128
                        elif o_glob < 4096:
                            dstT, off = kT, (o_glob - 2048) // 128
                        else:
                            dstT, off = vTt, (o_glob - 4096) // 128
                        p.o("scalar", lambda s, dstT=dstT, off=off, bank=bank: s.activation(
                            dstT[:, off * 512:(off + 1) * 512],
                            ps[:, bank * 512:(bank + 1) * 512], AF.Silu),
                            w=[(S["mm"], mm_t)], i=[(S["sc"], 1)])
                        bank_free[bank] = (S["sc"], S["sc"].n)
                    wfree[buf] = (S["mm"], S["mm"].n)
                sc_cur = S["sc"].n
                p.o("sync", lambda e, sb=sb: e.dma_start(
                    out=qts.ap().rearrange("h p m -> p h m")[:, :, sb * 512:(sb + 1) * 512],
                    in_=qT.ap().rearrange("p (h m) -> p h m", h=NH)),
                    w=[(S["sc"], sc_cur)], i=[(S["dS"], 16)])
                p.o("sync", lambda e: e.wait_ge(S["dS"].h, S["dS"].n) or None)
                p.close()

                for bi in range(2):
                    blk = sb * 2 + bi
                    moff = bi * 256
                    p = Ph(nc)
                    for h in range(NH):
                        for ncc in range(2):
                            p.o("scalar", lambda s, h=h, ncc=ncc: s.activation(
                                ddT[:, ncc * 256:(ncc + 1) * 256],
                                ddf[:, ncc * 256:(ncc + 1) * 256], AF.Exp,
                                scale=slopeb[:, h:h + 1]),
                                w=[S["ev"].cur(), S["gp"].cur()] if ncc == 0 else (),
                                i=[(S["sc"], 1)])
                        p.o("scalar", lambda s, h=h: s.activation(
                            qdec[:, :], iot256[:, :], AF.Exp, scale=nslopeb[:, h:h + 1]),
                            w=[S["ev"].cur()], i=[(S["sc"], 1)])
                        sc_qd = S["sc"].n
                        for ncc in range(2):
                            p.o("tensor", lambda t, h=h, ncc=ncc, moff=moff: t.matmul(
                                ps[:, ncc * 256:(ncc + 1) * 256],
                                kT[:, h * 512 + moff + ncc * 128:
                                   h * 512 + moff + (ncc + 1) * 128],
                                qT[:, h * 512 + moff: h * 512 + moff + 256],
                                start=True, stop=True),
                                w=[S["ev"].cur(), S["gp"].cur()] if ncc == 0 else (),
                                i=[(S["mm"], 1)] if ncc == 1 else ())
                        mm_sT = S["mm"].n
                        for ncc in range(2):
                            p.o("tensor", lambda t, h=h, ncc=ncc, moff=moff: t.transpose(
                                psb[:, ncc * 128:(ncc + 1) * 128],
                                kT[:, h * 512 + moff + ncc * 128:
                                   h * 512 + moff + (ncc + 1) * 128],
                                identb[:, :]),
                                w=[S["ev"].cur()] if ncc == 0 else ())
                            p.o("tensor", lambda t, h=h, ncc=ncc, moff=moff: t.transpose(
                                psb[:, 256 + ncc * 128: 256 + (ncc + 1) * 128],
                                vTt[:, h * 512 + moff + ncc * 128:
                                    h * 512 + moff + (ncc + 1) * 128],
                                identb[:, :]),
                                i=[(S["mm"], 1)] if ncc == 1 else ())
                        mm_kv = S["mm"].n
                        for ncc in range(2):
                            p.o("vector", lambda v, h=h, ncc=ncc: v.tensor_scalar_mul(
                                ks[:, ncc * 128:(ncc + 1) * 128],
                                psb[:, ncc * 128:(ncc + 1) * 128],
                                kdT[:, h * 2 + ncc:h * 2 + ncc + 1]),
                                w=[(S["mm"], mm_kv)] if ncc == 0 else (), i=[(S["ev"], 1)])
                            p.o("vector", lambda v, ncc=ncc: v.tensor_copy(
                                vtm[:, ncc * 128:(ncc + 1) * 128],
                                psb[:, 256 + ncc * 128: 256 + (ncc + 1) * 128]),
                                i=[(S["ev"], 1)])
                        for ncc in range(2):
                            p.o("vector", lambda v, ncc=ncc: v.tensor_tensor(
                                sdT[:, ncc * 256:(ncc + 1) * 256],
                                ps[:, ncc * 256:(ncc + 1) * 256],
                                ddT[:, ncc * 256:(ncc + 1) * 256], ALU.mult),
                                w=[(S["mm"], mm_sT), (S["sc"], sc_qd), S["gp"].cur()]
                                if ncc == 0 else (),
                                i=[(S["ev"], 1)])
                        p.o("vector", lambda v, h=h, moff=moff: v.tensor_tensor(
                            qd[:, :], qT[:, h * 512 + moff: h * 512 + moff + 256],
                            qdec[:, :], ALU.mult),
                            w=[(S["sc"], sc_qd)], i=[(S["ev"], 1)])
                        ev_ops = S["ev"].n
                        p.o("tensor", lambda t, h=h: t.matmul(
                            ps[:, 1024:1280], stb[:, h * 128:(h + 1) * 128], qd[:, :],
                            start=True, stop=False),
                            w=[(S["ev"], ev_ops), S["sc"].cur()])
                        for ncc in range(2):
                            p.o("tensor", lambda t, ncc=ncc: t.matmul(
                                ps[:, 1024:1280], vtm[:, ncc * 128:(ncc + 1) * 128],
                                sdT[:, ncc * 256:(ncc + 1) * 256],
                                start=False, stop=(ncc == 1)),
                                i=[(S["mm"], 1)] if ncc == 1 else ())
                        for ncc in range(2):
                            p.o("tensor", lambda t, ncc=ncc: t.matmul(
                                ps[:, 1536:1664], ks[:, ncc * 128:(ncc + 1) * 128],
                                vtm[:, ncc * 128:(ncc + 1) * 128],
                                start=(ncc == 0), stop=(ncc == 1)),
                                i=[(S["mm"], 1)] if ncc == 1 else ())
                        mm_o = S["mm"].n
                        p.o("scalar", lambda s, h=h: s.activation(
                            ost[:, h * 256:(h + 1) * 256], ps[:, 1024:1280], AF.Copy),
                            w=[(S["mm"], mm_o)], i=[(S["sc"], 1)])
                        p.o("vector", lambda v, h=h: v.scalar_tensor_tensor(
                            state[:, h * 128:(h + 1) * 128], state[:, h * 128:(h + 1) * 128],
                            bdb[:, h:h + 1], ps[:, 1536:1664], ALU.mult, ALU.add),
                            w=[(S["mm"], mm_o)], i=[(S["ev"], 1)])
                        ev_st = S["ev"].n
                        p.o("scalar", lambda s, h=h: s.activation(
                            stb[:, h * 128:(h + 1) * 128], state[:, h * 128:(h + 1) * 128],
                            AF.Copy), w=[(S["ev"], ev_st)], i=[(S["sc"], 1)])
                    sc_cur = S["sc"].n
                    p.o("sync", lambda e, blk=blk: e.dma_start(
                        out=obuf_d.ap().rearrange("h p m -> p h m")[:, :, blk * 256:(blk + 1) * 256],
                        in_=ost.ap().rearrange("p (h m) -> p h m", h=NH)),
                        w=[(S["sc"], sc_cur)], i=[(S["dS"], 16)])
                    p.o("sync", lambda e: e.wait_ge(S["dS"].h, S["dS"].n) or None)
                    p.close()

            p = Ph(nc)
            ev_cur = S["ev"].n
            p.o("sync", lambda e: e.dma_start(
                out=agin.ap().rearrange("h p e -> p h e"),
                in_=state.ap().rearrange("p (h e) -> p h e", h=NH)),
                w=[(S["ev"], ev_cur)], i=[(S["dS"], 16)])
            dma_t = S["dS"].n
            p.o("gpsimd", lambda g: g.collective_compute(
                "AllGather", ALU.bypass,
                replica_groups=[[0, 1, 2, 3], [4, 5, 6, 7]],
                ins=[agin.ap().opt()], outs=[agout.ap().opt()]),
                w=[(S["dS"], dma_t)], i=[(S["cc"], 1)])
            p.close()

        with contextlib.ExitStack() as st:
            kvin = st.enter_context(nc.sbuf_tensor("ab_kv", [128, NH * 128], F32))
            kvb = st.enter_context(nc.sbuf_tensor("ab_kvb", [128, NH * 128], BF16))
            aag = st.enter_context(nc.sbuf_tensor("ab_aag", [128, NH * 128], F32))
            wjb = st.enter_context(nc.sbuf_tensor("ab_wjb", [128, 48], F32))
            ebm = st.enter_context(nc.sbuf_tensor("ab_ebm", [128, 8], F32))
            obh = st.enter_context(nc.sbuf_tensor("ab_obh", [128, 1024], BF16))
            qth = st.enter_context(nc.sbuf_tensor("ab_qth", [128, 1024], BF16))
            iot1k = st.enter_context(nc.sbuf_tensor("ab_iot1k", [128, 1024], F32))
            cdb = st.enter_context(nc.sbuf_tensor("ab_cdb", [128, 1024], F32))
            qs = st.enter_context(nc.sbuf_tensor("ab_qs", [128, 1024], BF16))
            otot = st.enter_context(nc.sbuf_tensor("ab_ot", [128, 1024], BF16))
            sqh = st.enter_context(nc.sbuf_tensor("ab_sq", [128, 1024], BF16))
            god = st.enter_context(nc.sbuf_tensor("ab_god", [128, NH * 1024], BF16))
            xT2 = st.enter_context(nc.sbuf_tensor("ab_xT2", [128, FC * 1024], BF16))
            xlnb2 = st.enter_context(nc.sbuf_tensor("ab_xlnb2", [128, D], BF16))
            xlnb2b = st.enter_context(nc.sbuf_tensor("ab_xlnb2b", [128, D], BF16))
            gts = st.enter_context(nc.sbuf_tensor("ab_gts", [128, 1024], BF16))
            vrow = st.enter_context(nc.sbuf_tensor("ab_vrow", [1, 1024], F32))
            rstb = st.enter_context(nc.sbuf_tensor("ab_rstb", [128, 1024], F32))
            wbuf2 = st.enter_context(nc.sbuf_tensor("ab_w2", [128, FC * 512], BF16))
            ps = st.enter_context(nc.psum_tensor("ab_ps2", [128, 3072], F32))
            psb = st.enter_context(nc.psum_tensor("ab_psb2", [128, 2048], BF16))

            p = Ph(nc)
            dma_t = S["dS"].n + 32
            p.o("sync", lambda e: e.dma_start(out=ebm[:, 0:3], in_=agw_e[:, :]),
                i=[(S["dS"], 16)])
            p.o("sync", lambda e: e.dma_start(out=ebm[:, 4:7], in_=agw_m[:, :]),
                i=[(S["dS"], 16)])
            p.o("gpsimd", lambda g: g.engine_nop(), w=[(S["dS"], dma_t)], i=[(S["gp"], 1)])
            gp_t = S["gp"].n
            p.o("gpsimd", lambda g: g.iota(iot1k[:, :], [[1, 1024]], base=1,
                                           channel_multiplier=0,
                                           allow_small_or_imprecise_dtypes=True),
                i=[(S["gp"], 1)])
            for h in range(NH):
                p.o("vector", lambda v, h=h: v.tensor_scalar_mul(
                    wjb[:, h * 3:(h + 1) * 3], ebm[:, 0:3], slopeb[:, h:h + 1]),
                    w=[(S["gp"], gp_t), (S["dS"], dma_t)] if h == 0 else (), i=[(S["ev"], 1)])
            ev_t = S["ev"].n
            for h in range(NH):
                p.o("scalar", lambda s, h=h: s.activation(
                    wjb[:, h * 3:(h + 1) * 3], wjb[:, h * 3:(h + 1) * 3], AF.Exp,
                    scale=-1024.0),
                    w=[(S["ev"], ev_t)] if h == 0 else (), i=[(S["sc"], 1)])
            sc_t = S["sc"].n
            for h in range(NH):
                p.o("vector", lambda v, h=h: v.tensor_mul(
                    wjb[:, h * 3:(h + 1) * 3], wjb[:, h * 3:(h + 1) * 3], ebm[:, 4:7]),
                    w=[(S["sc"], sc_t)] if h == 0 else (), i=[(S["ev"], 1)])
            p.o("gpsimd", lambda g: g.memset(kvin[:, :], 0.0), i=[(S["gp"], 1)])
            gp_t = S["gp"].n
            cc_t = S["cc"].n
            for j in range(3):
                dma_t = S["dS"].n + 16
                p.o("sync", lambda e, j=j: e.dma_start(
                    out=aag.ap().rearrange("p (h e) -> p h e", h=NH),
                    in_=agout.ap()[j].rearrange("h p e -> p h e")),
                    w=[(S["cc"], cc_t), S["ev"].cur()], i=[(S["dS"], 16)])
                for h in range(NH):
                    p.o("vector", lambda v, h=h, j=j: v.scalar_tensor_tensor(
                        kvin[:, h * 128:(h + 1) * 128], aag[:, h * 128:(h + 1) * 128],
                        wjb[:, h * 3 + j:h * 3 + j + 1], kvin[:, h * 128:(h + 1) * 128],
                        ALU.mult, ALU.add),
                        w=[(S["dS"], dma_t), (S["gp"], gp_t)], i=[(S["ev"], 1)])
            ev_kv = S["ev"].n
            p.o("scalar", lambda s: s.activation(kvb[:, :], kvin[:, :], AF.Copy),
                w=[(S["ev"], ev_kv)], i=[(S["sc"], 1)])
            p.close()

            p = Ph(nc)
            sc_kvb = S["sc"].n
            for h in range(NH):
                dma_o = S["dS"].n + 16
                p.o("sync", lambda e, h=h: e.dma_start(out=obh[:, :], in_=obuf_d.ap()[h]),
                    w=[S["ev"].cur()], i=[(S["dS"], 16)])
                dma_q = S["dS"].n + 16
                p.o("sync", lambda e, h=h: e.dma_start(out=qth[:, :], in_=qts.ap()[h]),
                    w=[S["gp"].cur()], i=[(S["dS"], 16)])
                p.o("scalar", lambda s, h=h: s.activation(
                    cdb[:, :], iot1k[:, :], AF.Exp, scale=nslopeb[:, h:h + 1]),
                    w=[S["gp"].cur()], i=[(S["sc"], 1)])
                sc_cd = S["sc"].n
                p.o("gpsimd", lambda g: g.tensor_mul(qs[:, :], qth[:, :], cdb[:, :]),
                    w=[(S["dS"], dma_q), (S["mm"], S["mm"].n), (S["sc"], sc_cd)],
                    i=[(S["gp"], 1)])
                gp_qs = S["gp"].n
                for mh in range(2):
                    p.o("tensor", lambda t, h=h, mh=mh: t.matmul(
                        ps[:, 1024 + mh * 512:1024 + (mh + 1) * 512],
                        kvb[:, h * 128:(h + 1) * 128], qs[:, mh * 512:(mh + 1) * 512],
                        start=True, stop=True),
                        w=[(S["gp"], gp_qs), (S["sc"], sc_kvb), S["ev"].cur()]
                        if mh == 0 else (),
                        i=[(S["mm"], 1)] if mh == 1 else ())
                mm_t = S["mm"].n
                p.o("vector", lambda v: v.tensor_tensor(
                    otot[:, :], obh[:, :], ps[:, 1024:2048], ALU.add),
                    w=[(S["mm"], mm_t), (S["dS"], dma_o), S["sc"].cur(),
                       (S["dS"], S["dS"].n)], i=[(S["ev"], 1)])
                ev_ot = S["ev"].n
                p.o("sync", lambda e, h=h: e.dma_start(out=obuf_d.ap()[h], in_=otot[:, :]),
                    w=[(S["ev"], ev_ot)], i=[(S["dS"], 16)])
                p.o("scalar", lambda s: s.activation(sqh[:, :], otot[:, :], AF.Square),
                    w=[(S["ev"], ev_ot)], i=[(S["sc"], 1)])
                sc_t = S["sc"].n
                for mh in range(2):
                    p.o("tensor", lambda t, h=h, mh=mh: t.matmul(
                        ps[0:1, 2048 + mh * 512:2048 + (mh + 1) * 512],
                        onesb[:, 0:1], sqh[:, mh * 512:(mh + 1) * 512],
                        start=(h == 0), stop=(h == NH - 1)),
                        w=[(S["sc"], sc_t)] if mh == 0 else (),
                        i=[(S["mm"], 1)] if (mh == 1 and h == NH - 1) else ())
            mm_var = S["mm"].n
            p.o("sync", lambda e: e.wait_ge(S["dS"].h, S["dS"].n) or None)
            p.o("vector", lambda v: v.tensor_scalar(
                vrow[0:1, :], ps[0:1, 2048:3072], 1.0 / D, EPS_RMS, ALU.mult, ALU.add),
                w=[(S["mm"], mm_var)], i=[(S["ev"], 1)])
            p.o("vector", lambda v: v.reciprocal(vrow[0:1, :], vrow[0:1, :]),
                w=[S["ev"].cur()], i=[(S["ev"], 1)])
            ev_t = S["ev"].n
            p.o("scalar", lambda s: s.activation(vrow[0:1, :], vrow[0:1, :], AF.Sqrt),
                w=[(S["ev"], ev_t)], i=[(S["sc"], 1)])
            sc_t = S["sc"].n
            for mh in range(2):
                p.o("tensor", lambda t, mh=mh: t.matmul(
                    ps[:, mh * 512:(mh + 1) * 512], onesrf[0:1, :],
                    vrow[0:1, mh * 512:(mh + 1) * 512], start=True, stop=True),
                    w=[(S["sc"], sc_t)] if mh == 0 else (),
                    i=[(S["mm"], 1)] if mh == 1 else ())
            mm_t = S["mm"].n
            p.o("vector", lambda v: v.tensor_copy(rstb[:, :], ps[:, 0:1024]),
                w=[(S["mm"], mm_t)], i=[(S["ev"], 1)])
            p.o("gpsimd", lambda g: g.engine_nop(), w=[S["ev"].cur()], i=[(S["gp"], 1)])
            p.close()

            p = Ph(nc)
            srcs = [R[:, wt * D:(wt + 1) * D] for wt in range(8)]
            ln_xt_build(p, srcs, xT2, 1024, psb, [xlnb2, xlnb2b], t0=0)
            p.close()

            p = Ph(nc)
            bank_free = {b: (S["sc"], 0) for b in range(2)}
            wfree = (S["mm"], 0)
            gts_free = (S["gp"], 0)
            otot_free = (S["gp"], 0)
            for j in range(4):
                dma_t = S["dG"].n + 16
                p.o("gpsimd", lambda e, j=j: e.dma_start(
                    out=wbuf2.ap().rearrange("p (c o) -> p c o", c=FC),
                    in_=wg_d.ap()[j]),
                    w=[wfree], i=[(S["dG"], 16)])
                for oc in range(4):
                    h = j * 4 + oc
                    bank = oc % 2
                    for mh in range(2):
                        for c in range(FC):
                            p.o("tensor", lambda t, c=c, oc=oc, mh=mh, bank=bank: t.matmul(
                                ps[:, bank * 1024 + mh * 512: bank * 1024 + (mh + 1) * 512],
                                wbuf2[:, c * 512 + oc * 128: c * 512 + oc * 128 + 128],
                                xT2[:, c * 1024 + mh * 512: c * 1024 + (mh + 1) * 512],
                                start=(c == 0), stop=(c == FC - 1)),
                                w=[(S["dG"], dma_t), bank_free[bank],
                                   S["ev"].cur() if (j, oc) == (0, 0) else (S["ev"], 0)]
                                if c == 0 else (),
                                i=[(S["mm"], 1)] if c == FC - 1 else ())
                    mm_t = S["mm"].n
                    p.o("scalar", lambda s, bank=bank: s.activation(
                        gts[:, :], ps[:, bank * 1024:(bank + 1) * 1024], AF.Sigmoid),
                        w=[(S["mm"], mm_t), gts_free], i=[(S["sc"], 1)])
                    bank_free[bank] = (S["sc"], S["sc"].n)
                    sc_t = S["sc"].n
                    dma_o = S["dS"].n + 16
                    p.o("sync", lambda e, h=h: e.dma_start(out=obh[:, :], in_=obuf_d.ap()[h]),
                        w=[S["ev"].cur()], i=[(S["dS"], 16)])
                    p.o("vector", lambda v, h=h: v.scalar_tensor_tensor(
                        otot[:, :], obh[:, :], normw_s[:, h:h + 1], rstb[:, :],
                        ALU.mult, ALU.mult),
                        w=[(S["dS"], dma_o), otot_free, S["gp"].cur()], i=[(S["ev"], 1)])
                    ev_t = S["ev"].n
                    p.o("gpsimd", lambda g, h=h: g.tensor_mul(
                        god[:, h * 1024:(h + 1) * 1024], gts[:, :], otot[:, :]),
                        w=[(S["ev"], ev_t), (S["sc"], sc_t)], i=[(S["gp"], 1)])
                    gts_free = (S["gp"], S["gp"].n)
                    otot_free = (S["gp"], S["gp"].n)
                wfree = (S["mm"], S["mm"].n)
            p.close()

            p = Ph(nc)
            gp_god = S["gp"].n
            bank_free = {b: (S["ev"], 0) for b in range(4)}
            wfree = (S["mm"], 0)
            for fb in range(4):
                dma_t = S["dG"].n + 16
                p.o("gpsimd", lambda e, fb=fb: e.dma_start(
                    out=wbuf2.ap().rearrange("p (c o) -> p c o", c=FC),
                    in_=wo2_d.ap()[fb]),
                    w=[wfree], i=[(S["dG"], 16)])
                for qt in range(8):
                    bank = qt % 4
                    for h in range(NH):
                        p.o("tensor", lambda t, h=h, qt=qt, bank=bank: t.matmul(
                            ps[:, bank * 512:(bank + 1) * 512],
                            god[:, h * 1024 + qt * 128: h * 1024 + (qt + 1) * 128],
                            wbuf2[:, h * 512:(h + 1) * 512],
                            start=(h == 0), stop=(h == NH - 1)),
                            w=[(S["dG"], dma_t), (S["gp"], gp_god), bank_free[bank]]
                            if h == 0 else (),
                            i=[(S["mm"], 1)] if h == NH - 1 else ())
                    mm_t = S["mm"].n
                    p.o("vector", lambda v, qt=qt, fb=fb, bank=bank: v.tensor_tensor(
                        R[:, qt * D + fb * 512: qt * D + (fb + 1) * 512],
                        R[:, qt * D + fb * 512: qt * D + (fb + 1) * 512],
                        ps[:, bank * 512:(bank + 1) * 512], ALU.add),
                        w=[(S["mm"], mm_t)], i=[(S["ev"], 1)])
                    bank_free[bank] = (S["ev"], S["ev"].n)
                wfree = (S["mm"], S["mm"].n)
            p.close()

    if n_sub >= 2:
        ffn_ternarize()
    if n_sub >= 1:
        attn_a()
    if n_sub >= 2:
        ffn(0)
    if n_sub >= 3:
        attn_b()
    if n_sub >= 4:
        ffn(1)

    p = Ph(nc)
    ev_cur = S["ev"].n
    p.o("sync", lambda e: e.dma_start(
        out=y_out.ap().rearrange("(t p) f -> p t f", p=128),
        in_=R.ap().rearrange("p (t f) -> p t f", t=NT)),
        w=[(S["ev"], ev_cur)], i=[(S["out"], 16)])
    p.o("sync", lambda e: e.wait_ge(S["out"].h, S["out"].n) or None)
    p.close()

    return nc


def _blk_fco(w):
    w = np.asarray(w, np.float32)
    F, O = w.shape
    v = w.reshape(F // 128, 128, O // 512, 512)
    return np.ascontiguousarray(v.transpose(2, 1, 0, 3))


def _prep_inputs(inputs):
    T = np.ascontiguousarray
    d = {}
    d["wq"] = _blk_fco(inputs["L0_attn_a_Wq"])
    d["wk"] = _blk_fco(inputs["L0_attn_a_Wk"])
    d["wv"] = _blk_fco(inputs["L0_attn_a_Wv"])
    d["wo"] = _blk_fco(inputs["L0_attn_a_Wo"])
    w = np.asarray(inputs["L0_attn_b_w_qkv"], np.float32).reshape(16, 3, 128, 2048)
    d["wqkv_r"] = _blk_fco(T(w.transpose(1, 0, 2, 3).reshape(6144, 2048).T))
    d["wg_t"] = _blk_fco(np.asarray(inputs["L0_attn_b_w_gate"], np.float32).T)
    d["wout2"] = _blk_fco(np.asarray(inputs["L0_attn_b_w_out"], np.float32).T)
    d["normw"] = inputs["L0_attn_b_norm_w"]
    d["slope"] = T(np.broadcast_to(
        np.asarray(inputs["L0_attn_b_slope_rate"], np.float32).reshape(1, 16), (128, 16)))
    d["ob_r"] = T(np.broadcast_to(
        np.asarray(inputs["L0_ffn_out_b"], np.float32).reshape(1, 2048), (128, 2048)))
    d["gate_b"] = inputs["L0_ffn_gate_b"]
    d["val_b"] = inputs["L0_ffn_val_b"]
    d["out_b"] = inputs["L0_ffn_out_b"]
    bg = _blk_fco(np.asarray(inputs["L0_ffn_gate_w"], np.float32).T)
    bv = _blk_fco(np.asarray(inputs["L0_ffn_val_w"], np.float32).T)
    wof = np.asarray(inputs["L0_ffn_out_w"], np.float32).T
    v = wof.reshape(64, 128, 4, 512)
    bo = np.ascontiguousarray(v.transpose(2, 0, 1, 3))
    return d, bg, bv, bo.reshape(256, 128, 512)


def make_in_maps(inputs):
    T = np.ascontiguousarray
    shared, bg, bv, bo = _prep_inputs(inputs)
    x = inputs["L0_attn_a_x"]
    in_maps = []
    for c in range(8):
        b, pc = c // 4, c % 4
        m = dict(shared)
        m["wg_sh"] = T(bg[2 * c:2 * c + 2])
        m["wv_sh"] = T(bv[2 * c:2 * c + 2])
        m["wo_sh"] = T(bo[32 * c:32 * c + 32])
        m["x_own"] = T(np.asarray(x[b, pc * 1024:(pc + 1) * 1024], np.float32))
        if pc == 0:
            m["x_prev"] = np.zeros((512, D), np.float32)
            m["pmrow"] = np.full((1, 512), NEG, np.float32)
        else:
            m["x_prev"] = T(np.asarray(x[b, pc * 1024 - 512:pc * 1024], np.float32))
            m["pmrow"] = np.zeros((1, 512), np.float32)
        e = np.zeros((128, 3), np.float32)
        mk = np.zeros((128, 3), np.float32)
        for j in range(3):
            if j < pc:
                e[:, j] = pc - 1 - j
                mk[:, j] = 1.0
        m["agw_e"] = e
        m["agw_m"] = mk
        in_maps.append(m)
    return in_maps


def kernel(**inputs):
    in_maps = make_in_maps(inputs)
    nc = build_program(4)
    res = run_bass_kernel_spmd(nc, in_maps, core_ids=list(range(8)))
    out = np.zeros((2, 4096, D), np.float32)
    for c in range(8):
        b, pc = c // 4, c % 4
        out[b, pc * 1024:(pc + 1) * 1024] = res.results[c]["y"]
    return out


# version 13
# speedup vs baseline: 1.0108x; 1.0108x over previous
import contextlib
import numpy as np
import ml_dtypes
import concourse.bass as bass
import concourse.mybir as mybir
from concourse.bass_utils import run_bass_kernel_spmd

F32 = mybir.dt.float32
BF16 = mybir.dt.bfloat16
I8 = mybir.dt.int8
AF = mybir.ActivationFunctionType
ALU = mybir.AluOpType
AXX = mybir.AxisListType.X

D = 2048
NT = 8
FC = 16
MAGIC = 1.5 * 2.0 ** 23
EPS_LN = 1e-5
EPS_RMS = 1e-6
HID = 8192
NH = 16
NEG = -1e30
ENGINES = ("sync", "tensor", "vector", "scalar", "gpsimd")


class Sem:
    __slots__ = ("h", "n")

    def __init__(self, nc, name):
        self.h = nc.alloc_semaphore(name=name)
        self.n = 0

    def cur(self):
        return (self, self.n)


class Ph:

    def __init__(self, nc):
        self.nc = nc
        self.q = {e: [] for e in ENGINES}

    def o(self, eng, fn, w=(), i=()):
        ws = tuple((s.h, v) for (s, v) in w)
        incs = []
        for (s, k) in i:
            s.n += k
            incs.append((s.h, k))
        self.q[eng].append((fn, ws, tuple(incs)))

    def close(self):
        with self.nc.Block() as block:
            for e in ENGINES:
                items = self.q[e]
                if not items:
                    continue

                def body(eng, items=items):
                    for fn, ws, incs in items:
                        for h, v in ws:
                            eng.wait_ge(h, v)
                        inst = fn(eng)
                        for h, k in incs:
                            inst.then_inc(h, k)

                getattr(block, e)(body)
        self.q = None


def build_program(n_sub=4):
    nc = bass.Bass("TRN2", target_bir_lowering=False)

    x_own = nc.dram_tensor("x_own", [1024, D], F32, kind="ExternalInput")
    x_prev = nc.dram_tensor("x_prev", [512, D], F32, kind="ExternalInput")
    pmrow_in = nc.dram_tensor("pmrow", [1, 512], F32, kind="ExternalInput")
    agw_e = nc.dram_tensor("agw_e", [128, 3], F32, kind="ExternalInput")
    agw_m = nc.dram_tensor("agw_m", [128, 3], F32, kind="ExternalInput")
    wq_d = nc.dram_tensor("wq", [4, 128, FC, 512], F32, kind="ExternalInput")
    wk_d = nc.dram_tensor("wk", [1, 128, FC, 512], F32, kind="ExternalInput")
    wv_d = nc.dram_tensor("wv", [1, 128, FC, 512], F32, kind="ExternalInput")
    wo_d = nc.dram_tensor("wo", [4, 128, FC, 512], F32, kind="ExternalInput")
    wqkv_d = nc.dram_tensor("wqkv_r", [12, 128, FC, 512], F32, kind="ExternalInput")
    wg_d = nc.dram_tensor("wg_t", [4, 128, FC, 512], F32, kind="ExternalInput")
    wo2_d = nc.dram_tensor("wout2", [4, 128, FC, 512], F32, kind="ExternalInput")
    normw_in = nc.dram_tensor("normw", [2048], F32, kind="ExternalInput")
    slope_in = nc.dram_tensor("slope", [128, 16], F32, kind="ExternalInput")
    ob_r_in = nc.dram_tensor("ob_r", [128, D], F32, kind="ExternalInput")
    wg_sh = nc.dram_tensor("wg_sh", [2, 128, FC, 512], F32, kind="ExternalInput")
    wv_sh = nc.dram_tensor("wv_sh", [2, 128, FC, 512], F32, kind="ExternalInput")
    wo_sh = nc.dram_tensor("wo_sh", [32, 128, 512], F32, kind="ExternalInput")
    gb_d = nc.dram_tensor("gate_b", [HID], F32, kind="ExternalInput")
    vb_d = nc.dram_tensor("val_b", [HID], F32, kind="ExternalInput")
    ob_d = nc.dram_tensor("out_b", [D], F32, kind="ExternalInput")
    y_out = nc.dram_tensor("y", [1024, D], F32, kind="ExternalOutput")

    tern_g = nc.dram_tensor("tern_g", [16, 128, FC, 512], BF16, addr_space="Shared")
    tern_v = nc.dram_tensor("tern_v", [16, 128, FC, 512], BF16, addr_space="Shared")
    tern_o = nc.dram_tensor("tern_o", [4, 64, 128, 512], BF16, addr_space="Shared")
    tzg = nc.dram_tensor("tzg", [2, 128, FC, 512], BF16)
    tzv = nc.dram_tensor("tzv", [2, 128, FC, 512], BF16)
    tzo = nc.dram_tensor("tzo", [32, 128, 512], BF16)
    grin = nc.dram_tensor("grin", [1, 128], F32)
    grout = nc.dram_tensor("grout", [1, 128], F32)
    qts = nc.dram_tensor("qts", [NH, 128, 1024], BF16)
    obuf_d = nc.dram_tensor("obuf_d", [NH, 128, 1024], BF16)
    agin = nc.dram_tensor("agin", [NH, 128, 128], F32)
    agout = nc.dram_tensor("agout", [4, NH, 128, 128], F32)

    R = nc.alloc_sbuf_tensor("R", [128, NT * D], F32)
    identb = nc.alloc_sbuf_tensor("identb", [128, 128], BF16)
    identf = nc.alloc_sbuf_tensor("identf", [128, 128], F32)
    onesb = nc.alloc_sbuf_tensor("onesb", [128, 1], BF16)
    onesf = nc.alloc_sbuf_tensor("onesf", [128, 1], F32)
    onesrb = nc.alloc_sbuf_tensor("onesrb", [1, 128], BF16)
    onesrf = nc.alloc_sbuf_tensor("onesrf", [1, 128], F32)
    epsln = nc.alloc_sbuf_tensor("epsln", [128, 1], F32)
    magic = nc.alloc_sbuf_tensor("magic", [128, 1], F32)
    normw_s = nc.alloc_sbuf_tensor("normw_s", [128, 16], F32)
    slopeb = nc.alloc_sbuf_tensor("slopeb", [128, 16], F32)
    nslopeb = nc.alloc_sbuf_tensor("nslopeb", [128, 16], F32)
    stat = nc.alloc_sbuf_tensor("stat", [128, 32], F32)
    gamv = nc.alloc_sbuf_tensor("gamv", [128, 8], F32)
    gb_s = nc.alloc_sbuf_tensor("gb_s", [128, 64], F32)
    vb_s = nc.alloc_sbuf_tensor("vb_s", [128, 64], F32)
    ln_sq = nc.alloc_sbuf_tensor("ln_sq", [128, D], BF16)

    S = {k: Sem(nc, k) for k in ("dS", "dG", "dT", "dV", "dW", "dX", "dY", "dR",
                                 "mm", "ev", "sc", "gp", "aux", "out", "cc")}
    TERN_CC = [0]

    p = Ph(nc)
    p.o("scalar", lambda e: e.dma_start(
        out=R.ap().rearrange("p (t f) -> p t f", t=NT),
        in_=x_own.ap().rearrange("(t p) f -> p t f", p=128)), i=[(S["dR"], 16)])

    p.o("sync", lambda e: e.dma_start(
        out=normw_s[:, :], in_=normw_in.ap().rearrange("(h p) -> p h", p=128),
        allow_slow_non_contiguous=True),
        i=[(S["dS"], 16)])
    p.o("sync", lambda e: e.dma_start(out=slopeb[:, :], in_=slope_in[:, :]),
        i=[(S["dS"], 16)])
    p.o("sync", lambda e: e.dma_start(
        out=gb_s[:, :], in_=gb_d.ap().rearrange("(c p) -> p c", p=128),
        allow_slow_non_contiguous=True), i=[(S["dS"], 16)])
    p.o("sync", lambda e: e.dma_start(
        out=vb_s[:, :], in_=vb_d.ap().rearrange("(c p) -> p c", p=128),
        allow_slow_non_contiguous=True), i=[(S["dS"], 16)])

    dma_n = S["dS"].n
    p.o("gpsimd", lambda g: g.memset(identb[:, :], 0.0))
    p.o("gpsimd", lambda g: g.memset(identf[:, :], 0.0))
    p.o("gpsimd", lambda g: g.memset(onesb[:, :], 1.0))
    p.o("gpsimd", lambda g: g.memset(onesf[:, :], 1.0))
    p.o("gpsimd", lambda g: g.memset(epsln[:, :], EPS_LN))
    p.o("gpsimd", lambda g: g.memset(onesrb[:, :], 1.0))
    p.o("gpsimd", lambda g: g.memset(onesrf[:, :], 1.0))
    p.o("gpsimd", lambda g: g.memset(magic[:, :], MAGIC))
    p.o("gpsimd", lambda g: g.drain())
    p.o("gpsimd", lambda g: g.affine_select(
        out=identb[:, :], in_=identb[:, :], compare_op=ALU.not_equal, fill=1.0,
        base=0, pattern=[[-1, 128]], channel_multiplier=1))
    p.o("gpsimd", lambda g: g.affine_select(
        out=identf[:, :], in_=identf[:, :], compare_op=ALU.not_equal, fill=1.0,
        base=0, pattern=[[-1, 128]], channel_multiplier=1))
    p.o("vector", lambda v: v.tensor_scalar_mul(nslopeb[:, :], slopeb[:, :], -1.0),
        w=[(S["dS"], dma_n)], i=[(S["ev"], 1)])
    p.o("gpsimd", lambda g: g.drain(), w=[(S["dS"], dma_n), (S["dR"], 16)],
        i=[(S["gp"], 1)])
    p.close()

    def ln_tile(p, src_ap, dst_ap, w0=()):
        p.o("vector", lambda v: v.reduce_sum(stat[:, 0:1], src_ap, axis=AXX),
            w=w0, i=[(S["ev"], 1)])
        p.o("scalar", lambda s: s.activation(ln_sq[:, :], src_ap, AF.Square,
                                             accum_out=stat[:, 1:2]),
            w=w0, i=[(S["sc"], 1)])
        p.o("vector", lambda v: v.tensor_scalar_mul(stat[:, 2:3], stat[:, 0:1], 1.0 / D),
            w=[S["ev"].cur()], i=[(S["ev"], 1)])
        ev_mean = S["ev"].n
        p.o("scalar", lambda s: s.activation(stat[:, 3:4], stat[:, 2:3], AF.Square),
            w=[(S["ev"], ev_mean)], i=[(S["sc"], 1)])
        sc_msq = S["sc"].n
        p.o("vector", lambda v: v.scalar_tensor_tensor(
            stat[:, 3:4], stat[:, 1:2], 1.0 / D, stat[:, 3:4], ALU.mult, ALU.subtract),
            w=[(S["sc"], sc_msq)], i=[(S["ev"], 1)])
        ev_var = S["ev"].n
        p.o("scalar", lambda s: s.activation(stat[:, 3:4], stat[:, 3:4], AF.Ln,
                                             bias=epsln[:, 0:1]),
            w=[(S["ev"], ev_var)], i=[(S["sc"], 1)])
        p.o("scalar", lambda s: s.activation(stat[:, 4:5], stat[:, 3:4], AF.Exp, scale=-0.5),
            w=[S["sc"].cur()], i=[(S["sc"], 1)])
        sc_rstd = S["sc"].n
        p.o("vector", lambda v: v.tensor_scalar(
            dst_ap, src_ap, stat[:, 2:3], stat[:, 4:5], ALU.subtract, ALU.mult),
            w=[(S["sc"], sc_rstd), (S["mm"], S["mm"].n)], i=[(S["ev"], 1)])
        return S["ev"].n

    def ln_stats(p, srcs, w0s=None):
        T = len(srcs)
        sq_gate = S["sc"].cur()
        for t, src in enumerate(srcs):
            w0 = tuple(w0s[t]) if w0s else ()
            p.o("vector", lambda v, t=t, src=src: v.reduce_sum(
                stat[:, 8 + t:9 + t], src, axis=AXX), w=w0, i=[(S["ev"], 1)])
            p.o("scalar", lambda s, t=t, src=src: s.activation(
                ln_sq[:, :], src, AF.Square, accum_out=stat[:, 16 + t:17 + t]),
                w=w0 + (sq_gate,), i=[(S["sc"], 1)])
            sq_gate = S["sc"].cur()
        ev_t, sc_t = S["ev"].n, S["sc"].n
        p.o("vector", lambda v, T=T: v.tensor_scalar_mul(
            stat[:, 0:T], stat[:, 8:8 + T], 1.0 / D),
            w=[(S["ev"], ev_t), (S["sc"], sc_t)], i=[(S["ev"], 1)])
        ev_t = S["ev"].n
        p.o("scalar", lambda s, T=T: s.activation(
            stat[:, 24:24 + T], stat[:, 0:T], AF.Square),
            w=[(S["ev"], ev_t)], i=[(S["sc"], 1)])
        sc_t = S["sc"].n
        p.o("vector", lambda v, T=T: v.scalar_tensor_tensor(
            stat[:, 24:24 + T], stat[:, 16:16 + T], 1.0 / D, stat[:, 24:24 + T],
            ALU.mult, ALU.subtract),
            w=[(S["sc"], sc_t)], i=[(S["ev"], 1)])
        ev_t = S["ev"].n
        p.o("scalar", lambda s, T=T: s.activation(
            stat[:, 24:24 + T], stat[:, 24:24 + T], AF.Ln, bias=epsln[:, 0:1]),
            w=[(S["ev"], ev_t)], i=[(S["sc"], 1)])
        p.o("scalar", lambda s, T=T: s.activation(
            stat[:, 24:24 + T], stat[:, 24:24 + T], AF.Exp, scale=-0.5),
            w=[S["sc"].cur()], i=[(S["sc"], 1)])
        return (S["sc"], S["sc"].n)

    def ln_xt_build(p, srcs, dstT, stride, psb, xl2, w0s=None, t0=0):
        st_done = ln_stats(p, srcs, w0s)
        tfree = {0: (S["mm"], 0), 1: (S["mm"], 0)}
        for t, src in enumerate(srcs):
            b = t % 2
            p.o("vector", lambda v, t=t, src=src, b=b: v.tensor_scalar(
                xl2[b][:, :], src, stat[:, t:t + 1], stat[:, 24 + t:25 + t],
                ALU.subtract, ALU.mult),
                w=[st_done, tfree[b]], i=[(S["ev"], 1)])
            transpose_16(p, xl2[b], dstT, (t0 + t) * 128, stride, psb)
            tfree[b] = (S["mm"], S["mm"].n)

    def transpose_16(p, src_bf16, dst_tensor, dst_base, dst_tile_stride, psb):
        for half in range(2):
            for c8 in range(8):
                c = half * 8 + c8
                p.o("tensor", lambda t, c=c, c8=c8: t.transpose(
                    psb[:, c8 * 128:(c8 + 1) * 128],
                    src_bf16[:, c * 128:(c + 1) * 128], identb[:, :]),
                    w=[S["ev"].cur()] if c8 == 0 else (),
                    i=[(S["mm"], 1)] if c8 == 7 else ())
            mm_t = S["mm"].n
            dst = bass.AP(dst_tensor.ap().tensor,
                          (half * 8) * dst_tile_stride + dst_base,
                          [[dst_tensor.ap().ap[0][0], 128], [dst_tile_stride, 8], [1, 128]])
            srcp = bass.AP(psb.ap().tensor, 0,
                           [[psb.ap().ap[0][0], 128], [128, 8], [1, 128]])
            p.o("vector", lambda v, dst=dst, srcp=srcp: v.tensor_copy(dst, srcp),
                w=[(S["mm"], mm_t)], i=[(S["ev"], 1)])

    def attn_a():
        with contextlib.ExitStack() as st:
            xT = st.enter_context(nc.sbuf_tensor("aa_xT", [128, FC * 1024], BF16))
            xpst = st.enter_context(nc.sbuf_tensor("aa_xp", [128, D], F32))
            xlnb = st.enter_context(nc.sbuf_tensor("aa_xlnb", [128, D], BF16))
            xlnb_b = st.enter_context(nc.sbuf_tensor("aa_xlnbb", [128, D], BF16))
            qT = st.enter_context(nc.sbuf_tensor("aa_qT", [128, NH * 512], BF16))
            kT = st.enter_context(nc.sbuf_tensor("aa_kT", [128, 4 * 1024], BF16))
            vT = st.enter_context(nc.sbuf_tensor("aa_vT", [128, 4 * 1024], BF16))
            maskS = st.enter_context(nc.sbuf_tensor("aa_maskS", [128, 4 * 1024], BF16))
            pmrow = st.enter_context(nc.sbuf_tensor("aa_pmrow", [1, 512], BF16))
            p_sb = st.enter_context(nc.sbuf_tensor("aa_p", [128, 4 * 1024], BF16))
            pT = st.enter_context(nc.sbuf_tensor("aa_pT", [128, 8 * 512], BF16))
            oT = st.enter_context(nc.sbuf_tensor("aa_oT", [128, NH * 512], BF16))
            wbuf = st.enter_context(nc.sbuf_tensor("aa_w", [128, FC * 512], BF16))
            ps = st.enter_context(nc.psum_tensor("aa_ps", [128, 2560], F32))
            psb = st.enter_context(nc.psum_tensor("aa_psb", [128, 2048], BF16))

            p = Ph(nc)
            p.o("gpsimd", lambda g: g.dma_start(out=pmrow[0:1, :], in_=pmrow_in[0:1, :]),
                i=[(S["dG"], 16)])
            dma_pm = S["dG"].n
            for qt in range(4):
                p.o("gpsimd", lambda g, qt=qt: g.memset(
                    maskS[:, qt * 1024:(qt + 1) * 1024], 0.0))
            p.o("gpsimd", lambda g: g.drain())
            for qt in range(4):
                mprev = maskS[:, qt * 1024: qt * 1024 + 512]
                mcur = maskS[:, qt * 1024 + 512: qt * 1024 + 1024]
                p.o("gpsimd", lambda g, mprev=mprev, qt=qt: g.affine_select(
                    out=mprev, in_=mprev, compare_op=ALU.is_ge, fill=NEG,
                    base=-(qt * 128) - 1, channel_multiplier=-1, pattern=[[1, 512]]))
                p.o("gpsimd", lambda g, mcur=mcur, qt=qt: g.affine_select(
                    out=mcur, in_=mcur, compare_op=ALU.is_ge, fill=NEG,
                    base=qt * 128, channel_multiplier=1, pattern=[[-1, 512]]))
            p.o("gpsimd", lambda g: g.drain(), w=[(S["dG"], dma_pm)], i=[(S["gp"], 1)])
            gp_mask = S["gp"].n
            p.close()

            def ph_xt(blk):
                p = Ph(nc)
                if blk == 0:
                    for wt in range(4):
                        dma_t = S["dS"].n + 16
                        p.o("sync", lambda e, wt=wt: e.dma_start(
                            out=xpst[:, :],
                            in_=x_prev.ap().rearrange("(t p) f -> p t f", p=128)[:, wt, :]),
                            w=[S["aux"].cur()], i=[(S["dS"], 16)])
                        ev_dst = ln_tile(p, xpst[:, :], xlnb[:, :],
                                         w0=((S["dS"], dma_t),))
                        transpose_16(p, xlnb, xT, wt * 128, 1024, psb)
                        p.o("vector", lambda v: v.engine_nop(), w=[(S["ev"], ev_dst)],
                            i=[(S["aux"], 1)])
                    srcs = [R[:, t * D:(t + 1) * D] for t in range(4)]
                    ln_xt_build(p, srcs, xT, 1024, psb, [xlnb, xlnb_b], t0=4)
                else:
                    own = [0, 1, 2, 3, 4, 5, 6, 7]
                    srcs = [R[:, t * D:(t + 1) * D] for t in own]
                    ln_xt_build(p, srcs, xT, 1024, psb, [xlnb, xlnb_b], t0=0)
                p.close()

            def ph_qkv(blk):
                p = Ph(nc)
                bank_free = {b: (S["ev"], 0) for b in range(4)}
                wbuf_free = (S["mm"], 0)
                for j in range(4):
                    dma_t = S["dG"].n + 16
                    p.o("gpsimd", lambda e, j=j: e.dma_start(
                        out=wbuf.ap().rearrange("p (c o) -> p c o", c=FC),
                        in_=wq_d.ap()[j]),
                        w=[wbuf_free], i=[(S["dG"], 16)])
                    for oc in range(4):
                        h = j * 4 + oc
                        for c in range(FC):
                            p.o("tensor", lambda t, c=c, oc=oc: t.matmul(
                                ps[:, oc * 512:(oc + 1) * 512],
                                wbuf[:, c * 512 + oc * 128: c * 512 + oc * 128 + 128],
                                xT[:, c * 1024 + 512:(c + 1) * 1024],
                                start=(c == 0), stop=(c == FC - 1)),
                                w=[(S["dG"], dma_t), bank_free[oc]] if c == 0 else (),
                                i=[(S["mm"], 1)] if c == FC - 1 else ())
                        mm_t = S["mm"].n
                        p.o("vector", lambda v, oc=oc, h=h: v.tensor_scalar_mul(
                            qT[:, h * 512:(h + 1) * 512], ps[:, oc * 512:(oc + 1) * 512],
                            0.08838834764831845),
                            w=[(S["mm"], mm_t)], i=[(S["ev"], 1)])
                        bank_free[oc] = (S["ev"], S["ev"].n)
                    wbuf_free = (S["mm"], S["mm"].n)
                for wt_d, dstT in ((wk_d, kT), (wv_d, vT)):
                    dma_t = S["dG"].n + 16
                    p.o("gpsimd", lambda e, wt_d=wt_d: e.dma_start(
                        out=wbuf.ap().rearrange("p (c o) -> p c o", c=FC)[:, :, 0:512],
                        in_=wt_d.ap()[0]),
                        w=[wbuf_free], i=[(S["dG"], 16)])
                    for oc in range(4):
                        for mh in range(2):
                            for c in range(FC):
                                p.o("tensor", lambda t, c=c, oc=oc, mh=mh: t.matmul(
                                    ps[:, oc * 512:(oc + 1) * 512],
                                    wbuf[:, c * 512 + oc * 128: c * 512 + oc * 128 + 128],
                                    xT[:, c * 1024 + mh * 512: c * 1024 + (mh + 1) * 512],
                                    start=(c == 0), stop=(c == FC - 1)),
                                    w=[(S["dG"], dma_t), bank_free[oc]] if c == 0 else (),
                                    i=[(S["mm"], 1)] if c == FC - 1 else ())
                            mm_t = S["mm"].n
                            p.o("vector", lambda v, oc=oc, mh=mh, dstT=dstT: v.tensor_copy(
                                dstT[:, oc * 1024 + mh * 512: oc * 1024 + (mh + 1) * 512],
                                ps[:, oc * 512:(oc + 1) * 512]),
                                w=[(S["mm"], mm_t)], i=[(S["ev"], 1)])
                            bank_free[oc] = (S["ev"], S["ev"].n)
                    wbuf_free = (S["mm"], S["mm"].n)
                p.close()

            def ph_vtr(blk):
                p = Ph(nc)
                for hk in range(4):
                    for kc in range(8):
                        p.o("tensor", lambda t, hk=hk, kc=kc: t.transpose(
                            psb[:, kc * 128:(kc + 1) * 128],
                            vT[:, hk * 1024 + kc * 128: hk * 1024 + (kc + 1) * 128],
                            identb[:, :]),
                            w=[S["ev"].cur()] if kc == 0 else (),
                            i=[(S["mm"], 1)] if kc == 7 else ())
                    mm_t = S["mm"].n
                    p.o("vector", lambda v, hk=hk: v.tensor_copy(
                        vT[:, hk * 1024:(hk + 1) * 1024], psb[:, 0:1024]),
                        w=[(S["mm"], mm_t)], i=[(S["ev"], 1)])
                p.close()

            def ph_attn(blk):
                p = Ph(nc)
                bank_free = {b: (S["sc"], 0) for b in range(2)}
                for h in range(NH):
                    hk = h // 4
                    for qt in range(4):
                        bank = qt % 2
                        nmm = 5 if blk == 0 else 4
                        for half in range(2):
                            p.o("tensor", lambda t, h=h, hk=hk, qt=qt, half=half, bank=bank:
                                t.matmul(
                                    ps[:, bank * 1024 + half * 512:
                                       bank * 1024 + (half + 1) * 512],
                                    qT[:, h * 512 + qt * 128: h * 512 + (qt + 1) * 128],
                                    kT[:, hk * 1024 + half * 512: hk * 1024 + (half + 1) * 512],
                                    start=True, stop=False),
                                w=[bank_free[bank], (S["gp"], gp_mask),
                                   S["sc"].cur()] if half == 0 else ())
                            last = (half == 1) or (blk != 0)
                            p.o("tensor", lambda t, qt=qt, half=half, bank=bank:
                                t.matmul(
                                    ps[:, bank * 1024 + half * 512:
                                       bank * 1024 + (half + 1) * 512],
                                    identb[:, :],
                                    maskS[:, qt * 1024 + half * 512:
                                          qt * 1024 + (half + 1) * 512],
                                    start=False, stop=(half == 1 or blk != 0)),
                                i=[(S["mm"], 1)] if (half == 1) else ())
                            if half == 0 and blk == 0:
                                p.o("tensor", lambda t, bank=bank: t.matmul(
                                    ps[:, bank * 1024: bank * 1024 + 512],
                                    onesrb[0:1, :], pmrow[0:1, :],
                                    start=False, stop=True))
                        mm_t = S["mm"].n
                        p.o("vector", lambda v, qt=qt, bank=bank: v.reduce_max(
                            stat[:, 8 + qt:9 + qt], ps[:, bank * 1024:(bank + 1) * 1024],
                            axis=AXX, negate=True),
                            w=[(S["mm"], mm_t)], i=[(S["ev"], 1)])
                        ev_t = S["ev"].n
                        p.o("scalar", lambda s, qt=qt, bank=bank: s.activation(
                            p_sb[:, qt * 1024:(qt + 1) * 1024],
                            ps[:, bank * 1024:(bank + 1) * 1024],
                            AF.Exp, bias=stat[:, 8 + qt:9 + qt],
                            accum_out=stat[:, 12 + qt:13 + qt]),
                            w=[(S["ev"], ev_t), (S["mm"], S["mm"].n)], i=[(S["sc"], 1)])
                        bank_free[bank] = (S["sc"], S["sc"].n)
                        sc_t = S["sc"].n
                        p.o("vector", lambda v, qt=qt: v.reciprocal(
                            stat[:, 16 + qt:17 + qt], stat[:, 12 + qt:13 + qt]),
                            w=[(S["sc"], sc_t)], i=[(S["ev"], 1)])
                        ev_t = S["ev"].n
                        p.o("vector", lambda v, qt=qt: v.tensor_scalar_mul(
                            p_sb[:, qt * 1024:(qt + 1) * 1024],
                            p_sb[:, qt * 1024:(qt + 1) * 1024],
                            stat[:, 16 + qt:17 + qt]),
                            w=[(S["ev"], ev_t)], i=[(S["ev"], 1)])
                    ev_pnorm = S["ev"].n
                    for kc in range(8):
                        pb = kc % 2
                        for qt in range(4):
                            p.o("tensor", lambda t, kc=kc, qt=qt, pb=pb: t.transpose(
                                psb[:, pb * 1024 + qt * 128: pb * 1024 + (qt + 1) * 128],
                                p_sb[:, qt * 1024 + kc * 128: qt * 1024 + (kc + 1) * 128],
                                identb[:, :]),
                                w=[(S["ev"], ev_pnorm), S["ev"].cur()] if qt == 0 else (),
                                i=[(S["mm"], 1)] if qt == 3 else ())
                        mm_t = S["mm"].n
                        p.o("vector", lambda v, kc=kc, pb=pb: v.tensor_copy(
                            pT[:, kc * 512:(kc + 1) * 512],
                            psb[:, pb * 1024: pb * 1024 + 512]),
                            w=[(S["mm"], mm_t)], i=[(S["ev"], 1)])
                    ev_pT = S["ev"].n
                    for kc in range(8):
                        p.o("tensor", lambda t, hk=hk, kc=kc: t.matmul(
                            ps[:, 2048:2560],
                            vT[:, hk * 1024 + kc * 128: hk * 1024 + (kc + 1) * 128],
                            pT[:, kc * 512:(kc + 1) * 512],
                            start=(kc == 0), stop=(kc == 7)),
                            w=[(S["ev"], ev_pT), S["sc"].cur()] if kc == 0 else (),
                            i=[(S["mm"], 1)] if kc == 7 else ())
                    mm_av = S["mm"].n
                    p.o("scalar", lambda s, h=h: s.activation(
                        oT[:, h * 512:(h + 1) * 512], ps[:, 2048:2560], AF.Copy),
                        w=[(S["mm"], mm_av)], i=[(S["sc"], 1)])
                p.close()

            def ph_wo(blk):
                p = Ph(nc)
                bank_free = {b: (S["ev"], 0) for b in range(4)}
                wbuf_free = (S["mm"], 0)
                for fb in range(4):
                    dma_t = S["dG"].n + 16
                    p.o("gpsimd", lambda e, fb=fb: e.dma_start(
                        out=wbuf.ap().rearrange("p (c o) -> p c o", c=FC),
                        in_=wo_d.ap()[fb]),
                        w=[wbuf_free], i=[(S["dG"], 16)])
                    for qt in range(4):
                        t_own = blk * 4 + qt
                        bank = qt
                        for h in range(NH):
                            p.o("tensor", lambda t, h=h, qt=qt, bank=bank: t.matmul(
                                ps[:, bank * 512:(bank + 1) * 512],
                                oT[:, h * 512 + qt * 128: h * 512 + (qt + 1) * 128],
                                wbuf[:, h * 512: h * 512 + 512],
                                start=(h == 0), stop=(h == NH - 1)),
                                w=[(S["dG"], dma_t), bank_free[bank],
                                   S["sc"].cur() if h == 0 and qt == 0 and fb == 0
                                   else (S["sc"], 0)] if h == 0 else (),
                                i=[(S["mm"], 1)] if h == NH - 1 else ())
                        mm_t = S["mm"].n
                        p.o("vector", lambda v, t_own=t_own, fb=fb, bank=bank: v.tensor_tensor(
                            R[:, t_own * D + fb * 512: t_own * D + (fb + 1) * 512],
                            R[:, t_own * D + fb * 512: t_own * D + (fb + 1) * 512],
                            ps[:, bank * 512:(bank + 1) * 512], ALU.add),
                            w=[(S["mm"], mm_t)], i=[(S["ev"], 1)])
                        bank_free[bank] = (S["ev"], S["ev"].n)
                    wbuf_free = (S["mm"], S["mm"].n)
                p.close()

            ph_xt(0)
            ph_qkv(0)
            ph_vtr(0)
            ph_attn(0)
            ph_xt(1)
            ph_qkv(1)
            ph_vtr(1)
            ph_wo(0)
            ph_attn(1)
            ph_wo(1)
    def ffn_ternarize():
        with contextlib.ExitStack() as st:
            wst = st.enter_context(nc.sbuf_tensor("tz_w", [128, 2 * 4096], F32))
            acc = st.enter_context(nc.sbuf_tensor("tz_acc", [128, 16], F32))
            u_s = st.enter_context(nc.sbuf_tensor("tz_u", [128, 2 * 4096], F32))
            r_s = st.enter_context(nc.sbuf_tensor("tz_r", [128, 2 * 4096], F32))
            tb_s = st.enter_context(nc.sbuf_tensor("tz_t", [128, 2 * 4096], BF16))
            ps = st.enter_context(nc.psum_tensor("tz_ps", [128, 512], F32))

            CW = 4096
            NCH = 4
            NEL = 16 * 128 * FC * 512
            RG8 = [[0, 1, 2, 3, 4, 5, 6, 7]]

            def _flat(t):
                a = t.ap()
                s = "a b c d -> (a b c d)" if len(a.shape) == 4 else "a b c -> (a b c)"
                return a.rearrange(s).rearrange("(p x) -> p x", p=128)

            specs = [(_flat(wg_sh), _flat(tzg), 0),
                     (_flat(wv_sh), _flat(tzv), 1),
                     (_flat(wo_sh), _flat(tzo), 2)]

            p = Ph(nc)
            buf_free = {0: (S["sc"], 0), 1: (S["sc"], 0)}
            ci_all = 0
            for wflat, _tflat, gi in specs:
                for ci in range(NCH):
                    buf = ci_all % 2
                    bo = buf * 4096
                    dsem = S["dS"] if buf == 0 else S["dT"]
                    dma_t = dsem.n + 16
                    p.o("sync", lambda e, ci=ci, bo=bo, wflat=wflat: e.dma_start(
                        out=wst[:, bo: bo + CW],
                        in_=wflat[:, ci * CW:(ci + 1) * CW]),
                        w=[buf_free[buf]], i=[(dsem, 16)])
                    p.o("scalar", lambda s, bo=bo, gi=gi, ci=ci: s.activation(
                        u_s[:, bo: bo + CW], wst[:, bo: bo + CW], AF.Abs,
                        accum_out=acc[:, gi * 4 + ci:gi * 4 + ci + 1]),
                        w=[(dsem, dma_t)], i=[(S["sc"], 1)])
                    buf_free[buf] = (S["sc"], S["sc"].n)
                    ci_all += 1
            sc_all = S["sc"].n
            for gi in range(3):
                p.o("vector", lambda v, gi=gi: v.reduce_sum(
                    stat[:, gi:gi + 1], acc[:, gi * 4:gi * 4 + 4], axis=AXX),
                    w=[(S["sc"], sc_all)] if gi == 0 else (), i=[(S["ev"], 1)])
            ev_t = S["ev"].n
            p.o("tensor", lambda t: t.matmul(ps[0:1, 0:3], onesf[:, 0:1], stat[:, 0:3],
                                             start=True, stop=True),
                w=[(S["ev"], ev_t)], i=[(S["mm"], 1)])
            mm_t = S["mm"].n
            p.o("gpsimd", lambda g: g.memset(u_s[0:1, 0:128], 0.0), i=[(S["gp"], 1)])
            gp_t = S["gp"].n
            p.o("vector", lambda v: v.tensor_copy(u_s[0:1, 0:3], ps[0:1, 0:3]),
                w=[(S["mm"], mm_t), (S["gp"], gp_t)], i=[(S["ev"], 1)])
            ev_t = S["ev"].n
            p.o("sync", lambda e: e.dma_start(out=grin[0:1, :], in_=u_s[0:1, 0:128]),
                w=[(S["ev"], ev_t)], i=[(S["dS"], 16)])
            dma_t = S["dS"].n
            p.o("gpsimd", lambda g: g.collective_compute(
                "AllReduce", ALU.add, replica_groups=RG8,
                ins=[grin.ap().opt()], outs=[grout.ap().opt()]),
                w=[(S["dS"], dma_t)], i=[(S["cc"], 1)])
            cc_ar = S["cc"].n
            p.o("sync", lambda e: e.dma_start(out=u_s[0:1, 128:256], in_=grout[0:1, :]),
                w=[(S["cc"], cc_ar)], i=[(S["dS"], 16)])
            dma_t = S["dS"].n
            p.o("vector", lambda v: v.tensor_scalar(
                stat[0:1, 12:15], u_s[0:1, 128:131], 1.0 / NEL, 1e-5,
                ALU.mult, ALU.max),
                w=[(S["dS"], dma_t), S["ev"].cur()], i=[(S["ev"], 1)])
            p.o("vector", lambda v: v.reciprocal(stat[0:1, 15:18], stat[0:1, 12:15]),
                w=[S["ev"].cur()], i=[(S["ev"], 1)])
            ev_t = S["ev"].n
            p.o("tensor", lambda t: t.matmul(
                ps[0:128, 8:14], onesrf[0:1, :], stat[0:1, 12:18],
                start=True, stop=True),
                w=[(S["ev"], ev_t)], i=[(S["mm"], 1)])
            mm_t = S["mm"].n
            p.o("vector", lambda v: v.tensor_copy(gamv[:, 3:6], ps[:, 8:11]),
                w=[(S["mm"], mm_t)], i=[(S["ev"], 1)])
            p.o("vector", lambda v: v.tensor_copy(gamv[:, 0:3], ps[:, 11:14]),
                i=[(S["ev"], 1)])
            ev_gam = S["ev"].n
            p.close()

            p = Ph(nc)
            wfree = {0: (S["sc"], 0), 1: (S["sc"], 0)}
            ufree = {0: (S["ev"], 0), 1: (S["ev"], 0)}
            sfree = {0: (S["ev"], 0), 1: (S["ev"], 0)}
            tfree = {0: (S["dW"], 0), 1: (S["dV"], 0)}
            tsem = {0: S["dW"], 1: S["dV"]}
            ci_all = 0
            for wflat, tflat, gi in specs:
                for ci in range(NCH):
                    buf = ci_all % 2
                    bo = buf * 4096
                    dsem = S["dS"] if buf == 0 else S["dT"]
                    dma_t = dsem.n + 16
                    p.o("sync", lambda e, ci=ci, bo=bo, wflat=wflat: e.dma_start(
                        out=wst[:, bo: bo + CW],
                        in_=wflat[:, ci * CW:(ci + 1) * CW]),
                        w=[wfree[buf]], i=[(dsem, 16)])
                    p.o("scalar", lambda s, bo=bo, gi=gi: s.activation(
                        u_s[:, bo: bo + CW], wst[:, bo: bo + CW], AF.Abs,
                        scale=gamv[:, gi:gi + 1]),
                        w=([(S["ev"], ev_gam)] if ci_all == 0 else [])
                        + [(dsem, dma_t), ufree[buf]], i=[(S["sc"], 1)])
                    p.o("scalar", lambda s, bo=bo: s.activation(
                        r_s[:, bo: bo + CW], wst[:, bo: bo + CW], AF.Sign),
                        w=[sfree[buf]], i=[(S["sc"], 1)])
                    wfree[buf] = (S["sc"], S["sc"].n)
                    sc_t = S["sc"].n
                    p.o("vector", lambda v, bo=bo: v.tensor_scalar(
                        u_s[:, bo: bo + CW], u_s[:, bo: bo + CW], 0.5, None, ALU.is_ge),
                        w=[(S["sc"], sc_t)], i=[(S["ev"], 1)])
                    p.o("vector", lambda v, bo=bo: v.tensor_tensor(
                        tb_s[:, bo: bo + CW], u_s[:, bo: bo + CW], r_s[:, bo: bo + CW],
                        ALU.mult),
                        w=[S["ev"].cur(), tfree[buf]], i=[(S["ev"], 1)])
                    ufree[buf] = (S["ev"], S["ev"].n)
                    sfree[buf] = (S["ev"], S["ev"].n)
                    ev_t = S["ev"].n
                    p.o("gpsimd", lambda g, ci=ci, bo=bo, tflat=tflat: g.dma_start(
                        out=tflat[:, ci * CW:(ci + 1) * CW], in_=tb_s[:, bo: bo + CW]),
                        w=[(S["ev"], ev_t)], i=[(tsem[buf], 16)])
                    tfree[buf] = (tsem[buf], tsem[buf].n)
                    ci_all += 1
            dW_f, dV_f = S["dW"].n, S["dV"].n
            for k, (tin, tout) in enumerate(
                    ((tzg, tern_g), (tzv, tern_v), (tzo, tern_o))):
                p.o("gpsimd", lambda g, tin=tin, tout=tout: g.collective_compute(
                    "AllGather", ALU.bypass, replica_groups=RG8,
                    ins=[tin.ap().opt()], outs=[tout.ap().opt()]),
                    w=[(S["dW"], dW_f), (S["dV"], dV_f)] if k == 0 else (),
                    i=[(S["cc"], 1)])
            TERN_CC[0] = S["cc"].n
            p.close()

    def ffn(call):
        sfx = str(call)
        with contextlib.ExitStack() as st:
            xln = st.enter_context(nc.sbuf_tensor("ff_xln" + sfx, [128, D], F32))
            xdq = st.enter_context(nc.sbuf_tensor("ff_xdq" + sfx, [128, D], BF16))
            xdqT = st.enter_context(nc.sbuf_tensor("ff_xdqT" + sfx, [128, FC * 512], BF16))
            gv = st.enter_context(nc.sbuf_tensor("ff_gv" + sfx, [128, 64 * 512], BF16))
            wb = st.enter_context(nc.sbuf_tensor("ff_wb" + sfx, [128, 2 * FC * 512], BF16))
            gt_s = st.enter_context(nc.sbuf_tensor("ff_g" + sfx, [128, 512], BF16))
            vt_s = st.enter_context(nc.sbuf_tensor("ff_v" + sfx, [128, 512], BF16))
            amax = st.enter_context(nc.sbuf_tensor("ff_amax" + sfx, [128, 512], BF16))
            amlo = st.enter_context(nc.sbuf_tensor("ff_amlo" + sfx, [128, 512], BF16))
            s2b = st.enter_context(nc.sbuf_tensor("ff_s2b" + sfx, [128, 512], F32))
            fi16 = st.enter_context(nc.sbuf_tensor("ff_i16" + sfx, [128, 512], mybir.dt.int16))
            fi16b = st.enter_context(nc.sbuf_tensor("ff_i16b" + sfx, [128, 512], mybir.dt.int16))
            dsx = st.enter_context(nc.sbuf_tensor("ff_dsx" + sfx, [128, 8], F32))
            csc = st.enter_context(nc.sbuf_tensor("ff_csc" + sfx, [128, 4], F32))
            obb = st.enter_context(nc.sbuf_tensor("ff_obb" + sfx, [128, D], BF16))
            ps = st.enter_context(nc.psum_tensor("ff_ps" + sfx, [128, 3072], F32))
            psb = st.enter_context(nc.psum_tensor("ff_psb" + sfx, [128, 2048], BF16))

            p = Ph(nc)
            p.o("gpsimd", lambda e: e.dma_start(out=obb[:, :], in_=ob_r_in[:, :]),
                i=[(S["dG"], 16)])
            p.o("gpsimd", lambda e: e.wait_ge(S["dG"].h, S["dG"].n) or None)
            p.close()

            for grp in range(2):
                p = Ph(nc)
                srcs = [R[:, (grp * 4 + qt) * D:(grp * 4 + qt + 1) * D] for qt in range(4)]
                st_done = ln_stats(p, srcs)
                for qt in range(4):
                    t_own = grp * 4 + qt
                    p.o("vector", lambda v, qt=qt, t_own=t_own: v.tensor_scalar(
                        xln[:, :], R[:, t_own * D:(t_own + 1) * D],
                        stat[:, qt:qt + 1], stat[:, 24 + qt:25 + qt],
                        ALU.subtract, ALU.mult),
                        w=[st_done, S["ev"].cur()], i=[(S["ev"], 1)])
                    ev_ln = S["ev"].n
                    p.o("vector", lambda v: v.reduce_max(
                        stat[:, 12:13], xln[:, :], axis=AXX, apply_absolute_value=True),
                        w=[(S["ev"], ev_ln)], i=[(S["ev"], 1)])
                    p.o("vector", lambda v: v.tensor_scalar_max(stat[:, 12:13], stat[:, 12:13], 1e-5),
                        w=[S["ev"].cur()], i=[(S["ev"], 1)])
                    p.o("vector", lambda v, qt=qt: v.tensor_scalar_mul(
                        dsx[:, 4 + qt:5 + qt], stat[:, 12:13], 1.0 / 127.0),
                        w=[S["ev"].cur()], i=[(S["ev"], 1)])
                    p.o("vector", lambda v: v.reciprocal(stat[:, 13:14], stat[:, 12:13]),
                        w=[S["ev"].cur()], i=[(S["ev"], 1)])
                    p.o("vector", lambda v: v.tensor_scalar_mul(stat[:, 13:14], stat[:, 13:14], 127.0),
                        w=[S["ev"].cur()], i=[(S["ev"], 1)])
                    for z in range(4):
                        zs = slice(z * 512, (z + 1) * 512)
                        p.o("vector", lambda v, zs=zs: v.tensor_scalar_mul(
                            fi16[:, :], xln[:, zs], stat[:, 13:14]),
                            w=[S["ev"].cur(), S["gp"].cur()], i=[(S["ev"], 1)])
                        p.o("vector", lambda v, qt=qt, zs=zs: v.tensor_scalar_mul(
                            xdq[:, zs], fi16[:, :], dsx[:, 4 + qt:5 + qt]),
                            w=[S["ev"].cur(), (S["mm"], S["mm"].n)], i=[(S["ev"], 1)])
                    transpose_16(p, xdq, xdqT, qt * 128, 512, psb)
                p.o("gpsimd", lambda g: g.memset(amax[:, :], -3e38), i=[(S["gp"], 1)])
                p.o("gpsimd", lambda g: g.memset(amlo[:, :], 3e38), i=[(S["gp"], 1)])
                p.close()

                p = Ph(nc)
                bank_free = {b: (S["sc"], 0) for b in range(4)}
                half_free = {k: ((S["mm"], 0), (S["mm"], 0)) for k in ("g", "v")}
                hsem = {("g", 0): S["dS"], ("g", 1): S["dT"],
                        ("v", 0): S["dX"], ("v", 1): S["dY"]}
                gt_free = (S["gp"], 0)
                vt_free = (S["gp"], 0)
                for j in range(16):
                    dma_t = {}
                    for sk, tern in (("g", tern_g), ("v", tern_v)):
                        for hf in range(2):
                            sem = hsem[(sk, hf)]
                            dma_t[(sk, hf)] = sem.n + 16
                            boff = (0 if sk == "g" else FC * 512) + hf * 4096
                            wl_ld = [half_free[sk][hf]]
                            if call == 0 and j == 0 and sk == "g" and hf == 0:
                                wl_ld.append((S["cc"], TERN_CC[0]))
                            p.o("sync", lambda e, j=j, tern=tern, boff=boff, hf=hf:
                                e.dma_start(
                                    out=wb[:, boff: boff + 4096],
                                    in_=tern.ap()[j].rearrange("p c o -> p (c o)")
                                    [:, hf * 4096:(hf + 1) * 4096]),
                                w=wl_ld, i=[(sem, 16)])
                    lo_rec = {}
                    hi_rec = {}
                    for oc in range(4):
                        hc = j * 4 + oc
                        gbank = oc % 2
                        vbank = 2 + oc % 2
                        for sk, bank in (("g", gbank), ("v", vbank)):
                            boff = 0 if sk == "g" else FC * 512
                            for c in range(FC):
                                wl = ()
                                if c == 0:
                                    wl = [(hsem[(sk, 0)], dma_t[(sk, 0)]),
                                          bank_free[bank]]
                                    if (j, oc) == (0, 0) and sk == "g":
                                        wl.append(S["ev"].cur())
                                elif c == 8:
                                    wl = [(hsem[(sk, 1)], dma_t[(sk, 1)])]
                                p.o("tensor", lambda t, c=c, oc=oc, bank=bank, boff=boff:
                                    t.matmul(
                                        ps[:, bank * 512:(bank + 1) * 512],
                                        wb[:, boff + c * 512 + oc * 128:
                                           boff + c * 512 + oc * 128 + 128],
                                        xdqT[:, c * 512:(c + 1) * 512],
                                        start=(c == 0), stop=(c == FC - 1)),
                                    w=wl,
                                    i=[(S["mm"], 1)] if c in (7, FC - 1) else ())
                                if c == 7:
                                    lo_rec[sk] = (S["mm"], S["mm"].n)
                                elif c == FC - 1:
                                    hi_rec[sk] = (S["mm"], S["mm"].n)
                        mm_t = S["mm"].n
                        p.o("scalar", lambda s, hc=hc, gbank=gbank: s.activation(
                            gt_s[:, :], ps[:, gbank * 512:(gbank + 1) * 512], AF.Silu,
                            scale=gamv[:, 3:4], bias=gb_s[:, hc:hc + 1]),
                            w=[(S["mm"], mm_t), gt_free, S["gp"].cur()
                               if (j, oc) == (0, 0) else (S["gp"], 0)],
                            i=[(S["sc"], 1)])
                        bank_free[gbank] = (S["sc"], S["sc"].n)
                        p.o("vector", lambda v, hc=hc, vbank=vbank: v.tensor_scalar(
                            vt_s[:, :], ps[:, vbank * 512:(vbank + 1) * 512],
                            gamv[:, 4:5], vb_s[:, hc:hc + 1], ALU.mult, ALU.add),
                            w=[vt_free, (S["mm"], S["mm"].n)], i=[(S["ev"], 1)])
                        bank_free[vbank] = (S["ev"], S["ev"].n)
                        sc_t = S["sc"].n
                        ev_t = S["ev"].n
                        p.o("gpsimd", lambda g, hc=hc: g.tensor_mul(
                            gv[:, hc * 512:(hc + 1) * 512], gt_s[:, :], vt_s[:, :]),
                            w=[(S["sc"], sc_t), (S["ev"], ev_t), S["gp"].cur()],
                            i=[(S["gp"], 1)])
                        gt_free = (S["gp"], S["gp"].n)
                        vt_free = (S["gp"], S["gp"].n)
                        p.o("vector", lambda v, hc=hc: v.tensor_tensor(
                            amax[:, :], amax[:, :], gv[:, hc * 512:(hc + 1) * 512],
                            ALU.max),
                            w=[S["gp"].cur(), S["ev"].cur()], i=[(S["ev"], 1)])
                        p.o("vector", lambda v, hc=hc: v.tensor_tensor(
                            amlo[:, :], amlo[:, :], gv[:, hc * 512:(hc + 1) * 512],
                            ALU.min),
                            w=[S["ev"].cur()], i=[(S["ev"], 1)])
                    half_free = {sk: (lo_rec[sk], hi_rec[sk]) for sk in ("g", "v")}
                p.close()

                p = Ph(nc)
                p.o("vector", lambda v: v.tensor_scalar_mul(amlo[:, :], amlo[:, :], -1.0),
                    w=[S["gp"].cur(), S["ev"].cur()], i=[(S["ev"], 1)])
                p.o("vector", lambda v: v.tensor_max(amax[:, :], amax[:, :], amlo[:, :]),
                    w=[S["ev"].cur()], i=[(S["ev"], 1)])
                ev_am = S["ev"].n
                for qt in range(4):
                    p.o("tensor", lambda t, qt=qt: t.transpose(
                        psb[:, qt * 512: qt * 512 + 128],
                        amax[:, qt * 128:(qt + 1) * 128], identb[:, :]),
                        w=[(S["ev"], ev_am)] if qt == 0 else (), i=[(S["mm"], 1)])
                    mm_t = S["mm"].n
                    p.o("vector", lambda v, qt=qt: v.reduce_max(
                        stat[:, 12 + qt:13 + qt], psb[:, qt * 512: qt * 512 + 128], axis=AXX),
                        w=[(S["mm"], mm_t)], i=[(S["ev"], 1)])
                for qt in range(4):
                    p.o("vector", lambda v, qt=qt: v.tensor_scalar_max(
                        stat[:, 12 + qt:13 + qt], stat[:, 12 + qt:13 + qt], 1e-5),
                        w=[S["ev"].cur()], i=[(S["ev"], 1)])
                for qt in range(4):
                    p.o("vector", lambda v, qt=qt: v.tensor_scalar_mul(
                        dsx[:, qt:qt + 1], stat[:, 12 + qt:13 + qt], 1.0 / 127.0),
                        w=[S["ev"].cur()], i=[(S["ev"], 1)])
                    p.o("vector", lambda v, qt=qt: v.reciprocal(
                        stat[:, 16 + qt:17 + qt], stat[:, 12 + qt:13 + qt]),
                        w=[S["ev"].cur()], i=[(S["ev"], 1)])
                for qt in range(4):
                    p.o("vector", lambda v, qt=qt: v.tensor_scalar_mul(
                        stat[:, 16 + qt:17 + qt], stat[:, 16 + qt:17 + qt], 127.0),
                        w=[S["ev"].cur()], i=[(S["ev"], 1)])
                    p.o("vector", lambda v, qt=qt: v.tensor_mul(
                        csc[:, qt:qt + 1], dsx[:, qt:qt + 1], gamv[:, 5:6]),
                        w=[S["ev"].cur()], i=[(S["ev"], 1)])
                ev_s = S["ev"].n
                for qt in range(4):
                    p.o("tensor", lambda t, qt=qt: t.transpose(
                        ps[0:1, 2048 + qt * 128: 2048 + (qt + 1) * 128],
                        stat[:, 16 + qt:17 + qt], identf[:, :]),
                        w=[(S["ev"], ev_s)] if qt == 0 else (),
                        i=[(S["mm"], 1)] if qt == 3 else ())
                mm_t = S["mm"].n
                p.o("vector", lambda v: v.tensor_copy(xln[0:1, 1024:1536], ps[0:1, 2048:2560]),
                    w=[(S["mm"], mm_t)], i=[(S["ev"], 1)])
                ev_t = S["ev"].n
                p.o("tensor", lambda t: t.matmul(
                    ps[:, 2560:3072], onesrf[0:1, :], xln[0:1, 1024:1536],
                    start=True, stop=True),
                    w=[(S["ev"], ev_t)], i=[(S["mm"], 1)])
                mm_s2 = S["mm"].n
                p.o("vector", lambda v: v.tensor_copy(s2b[:, :], ps[:, 2560:3072]),
                    w=[(S["mm"], mm_s2)], i=[(S["ev"], 1)])
                ev_s2 = S["ev"].n
                p.close()

                p = Ph(nc)
                gp_q = S["gp"].n
                fi16x = {0: fi16, 1: fi16b}
                fi_free = {0: (S["sc"], 0), 1: (S["sc"], 0)}
                idc = {}
                wob_free = {b: (S["mm"], 0) for b in range(4)}
                wob_sem = {0: S["dS"], 1: S["dT"], 2: S["dX"], 3: S["dY"]}
                bank_free = {b: (S["ev"], 0) for b in range(4)}
                for fb in range(4):
                    for hp in range(32):
                        buf = hp % 4
                        dsem = wob_sem[buf]
                        dma_t = dsem.n + 16
                        wob_a = xdqT[:, buf * 1024: buf * 1024 + 1024]
                        p.o("sync", lambda g, fb=fb, hp=hp, wob_a=wob_a: g.dma_start(
                            out=wob_a.rearrange("p (h o) -> p h o", h=2),
                            in_=tern_o.ap()[fb, 2 * hp:2 * hp + 2].rearrange(
                                "h p o -> p h o")),
                            w=[wob_free[buf]], i=[(dsem, 16)])
                        if fb == 0:
                            for h2 in range(2):
                                hc = 2 * hp + h2
                                qb = hc % 2
                                p.o("vector", lambda v, hc=hc, qb=qb: v.tensor_tensor(
                                    fi16x[qb][:, :],
                                    gv[:, hc * 512:(hc + 1) * 512], s2b[:, :],
                                    ALU.mult),
                                    w=([(S["ev"], ev_s2), (S["gp"], gp_q)]
                                       if hc == 0 else []) + [fi_free[qb]],
                                    i=[(S["ev"], 1)])
                                ev_t = S["ev"].n
                                p.o("scalar", lambda s, hc=hc, qb=qb: s.activation(
                                    gv[:, hc * 512:(hc + 1) * 512], fi16x[qb][:, :],
                                    AF.Identity),
                                    w=[(S["ev"], ev_t)], i=[(S["sc"], 1)])
                                fi_free[qb] = (S["sc"], S["sc"].n)
                                idc[hc] = S["sc"].n
                        for h2 in range(2):
                            hc = 2 * hp + h2
                            for qt in range(4):
                                wl = []
                                if qt == 0 and h2 == 0 and hp == 0:
                                    wl += [bank_free[b] for b in range(4)]
                                if qt == 0 and h2 == 0:
                                    wl.append((dsem, dma_t))
                                if qt == 0 and fb == 0:
                                    wl.append((S["sc"], idc[hc]))
                                p.o("tensor", lambda t, hc=hc, qt=qt, buf=buf, h2=h2:
                                    t.matmul(
                                        ps[:, qt * 512:(qt + 1) * 512],
                                        gv[:, hc * 512 + qt * 128:
                                           hc * 512 + (qt + 1) * 128],
                                        xdqT[:, buf * 1024 + h2 * 512:
                                             buf * 1024 + (h2 + 1) * 512],
                                        start=(hc == 0), stop=(hc == 63)),
                                    w=wl, i=[(S["mm"], 1)] if (qt == 3 and h2 == 1)
                                    else ())
                        wob_free[buf] = (S["mm"], S["mm"].n)
                    mm_t = S["mm"].n
                    for qt in range(4):
                        t_own = grp * 4 + qt
                        p.o("vector", lambda v, t_own=t_own, fb=fb, qt=qt:
                            v.scalar_tensor_tensor(
                                R[:, t_own * D + fb * 512: t_own * D + (fb + 1) * 512],
                                ps[:, qt * 512:(qt + 1) * 512], csc[:, qt:qt + 1],
                                R[:, t_own * D + fb * 512: t_own * D + (fb + 1) * 512],
                                ALU.mult, ALU.add),
                            w=[(S["mm"], mm_t), S["ev"].cur()] if qt == 0 else
                            [(S["mm"], mm_t)], i=[(S["ev"], 1)])
                        ev_t = S["ev"].n
                        p.o("vector", lambda v, t_own=t_own, fb=fb: v.tensor_tensor(
                            R[:, t_own * D + fb * 512: t_own * D + (fb + 1) * 512],
                            R[:, t_own * D + fb * 512: t_own * D + (fb + 1) * 512],
                            obb[:, fb * 512:(fb + 1) * 512], ALU.add),
                            w=[(S["ev"], ev_t)], i=[(S["ev"], 1)])
                        bank_free[qt] = (S["ev"], S["ev"].n)
                if call == 1:
                    ev_t = S["ev"].n
                    p.o("sync", lambda e, grp=grp: e.dma_start(
                        out=y_out.ap().rearrange("(t p) f -> p t f", p=128)
                        [:, grp * 4:(grp + 1) * 4, :],
                        in_=R.ap().rearrange("p (t f) -> p t f", t=NT)
                        [:, grp * 4:(grp + 1) * 4, :]),
                        w=[(S["ev"], ev_t)], i=[(S["out"], 16)])
                p.close()

    def attn_b():
        with contextlib.ExitStack() as st:
            xT = st.enter_context(nc.sbuf_tensor("ab_xT", [128, FC * 512], BF16))
            xlnb = st.enter_context(nc.sbuf_tensor("ab_xlnb", [128, D], BF16))
            xlnb_b = st.enter_context(nc.sbuf_tensor("ab_xlnbb", [128, D], BF16))
            qT = st.enter_context(nc.sbuf_tensor("ab_qT", [128, NH * 512], BF16))
            kT = st.enter_context(nc.sbuf_tensor("ab_kT", [128, NH * 512], BF16))
            vTt = st.enter_context(nc.sbuf_tensor("ab_vT", [128, NH * 512], BF16))
            ks = st.enter_context(nc.sbuf_tensor("ab_ks", [128, 256], BF16))
            vtm = st.enter_context(nc.sbuf_tensor("ab_vtm", [128, 256], BF16))
            sdT = st.enter_context(nc.sbuf_tensor("ab_sdT", [128, 512], BF16))
            qd = st.enter_context(nc.sbuf_tensor("ab_qd", [128, 256], BF16))
            ddT = st.enter_context(nc.sbuf_tensor("ab_ddT", [128, 512], BF16))
            ddf = st.enter_context(nc.sbuf_tensor("ab_ddf", [128, 512], BF16))
            qdec = st.enter_context(nc.sbuf_tensor("ab_qdec", [128, 256], F32))
            kdT = st.enter_context(nc.sbuf_tensor("ab_kdT", [128, 32], F32))
            iot256 = st.enter_context(nc.sbuf_tensor("ab_iot", [128, 256], F32))
            kiot = st.enter_context(nc.sbuf_tensor("ab_kiot", [128, 2], F32))
            c256 = st.enter_context(nc.sbuf_tensor("ab_c256", [128, 1], F32))
            bdb = st.enter_context(nc.sbuf_tensor("ab_bdb", [128, 16], F32))
            state = st.enter_context(nc.sbuf_tensor("ab_st", [128, NH * 128], F32))
            stb = st.enter_context(nc.sbuf_tensor("ab_stb", [128, NH * 128], BF16))
            ost = st.enter_context(nc.sbuf_tensor("ab_ost", [128, NH * 256], BF16))
            wbuf = st.enter_context(nc.sbuf_tensor("ab_w", [128, 2 * FC * 512], BF16))
            ps = st.enter_context(nc.psum_tensor("ab_ps", [128, 2048], F32))
            psb = st.enter_context(nc.psum_tensor("ab_psb", [128, 1024], BF16))

            p = Ph(nc)
            p.o("gpsimd", lambda g: g.iota(iot256[:, :], [[1, 256]], base=1,
                                           channel_multiplier=0,
                                           allow_small_or_imprecise_dtypes=True),
                i=[(S["gp"], 1)])
            for ncc in range(2):
                p.o("gpsimd", lambda g, ncc=ncc: g.iota(
                    kiot[:, ncc:ncc + 1], [[1, 1]], base=255 - ncc * 128,
                    channel_multiplier=-1, allow_small_or_imprecise_dtypes=True),
                    i=[(S["gp"], 1)])
            p.o("gpsimd", lambda g: g.memset(c256[:, :], 256.0), i=[(S["gp"], 1)])
            gp_t = S["gp"].n
            for h in range(NH):
                p.o("scalar", lambda s, h=h: s.activation(
                    bdb[:, h:h + 1], c256[:, 0:1], AF.Exp, scale=nslopeb[:, h:h + 1]),
                    w=[(S["gp"], gp_t)] if h == 0 else (), i=[(S["sc"], 1)])
                for ncc in range(2):
                    p.o("scalar", lambda s, h=h, ncc=ncc: s.activation(
                        kdT[:, h * 2 + ncc:h * 2 + ncc + 1], kiot[:, ncc:ncc + 1],
                        AF.Exp, scale=nslopeb[:, h:h + 1]), i=[(S["sc"], 1)])
            for ncc in range(2):
                p.o("gpsimd", lambda g, ncc=ncc: g.iota(
                    ddf[:, ncc * 256:(ncc + 1) * 256], [[-1, 256]], base=ncc * 128,
                    channel_multiplier=1, allow_small_or_imprecise_dtypes=True),
                    i=[(S["gp"], 1)])
                p.o("gpsimd", lambda g, ncc=ncc: g.affine_select(
                    out=ddf[:, ncc * 256:(ncc + 1) * 256],
                    in_=ddf[:, ncc * 256:(ncc + 1) * 256],
                    compare_op=ALU.is_ge, fill=NEG,
                    base=-(ncc * 128), channel_multiplier=-1, pattern=[[1, 256]]),
                    w=[S["gp"].cur()], i=[(S["gp"], 1)])
            p.o("gpsimd", lambda g: g.memset(state[:, :], 0.0), i=[(S["gp"], 1)])
            p.o("gpsimd", lambda g: g.memset(stb[:, :], 0.0), i=[(S["gp"], 1)])
            p.close()

            for sb in range(2):
                p = Ph(nc)
                srcs = [R[:, (sb * 4 + wt) * D:(sb * 4 + wt + 1) * D] for wt in range(4)]
                ln_xt_build(p, srcs, xT, 512, psb, [xlnb, xlnb_b], t0=0)
                p.close()

                p = Ph(nc)
                bank_free = {b: (S["sc"], 0) for b in range(4)}
                wfree = {0: (S["dS"], 0), 1: (S["dS"], 0)}
                for j in range(12):
                    buf = j % 2
                    dsem = S["dG"] if buf == 0 else S["dV"]
                    dma_t = dsem.n + 16
                    p.o("gpsimd", lambda g, j=j, buf=buf: g.dma_start(
                        out=wbuf.ap().rearrange("p (b c o) -> p b c o", b=2, c=FC)[:, buf],
                        in_=wqkv_d.ap()[j]),
                        w=[wfree[buf]], i=[(dsem, 16)])
                    for oc in range(4):
                        o_glob = j * 512 + oc * 128
                        bank = oc
                        for c in range(FC):
                            p.o("tensor", lambda t, c=c, oc=oc, buf=buf, bank=bank: t.matmul(
                                ps[:, bank * 512:(bank + 1) * 512],
                                wbuf[:, buf * FC * 512 + c * 512 + oc * 128:
                                     buf * FC * 512 + c * 512 + oc * 128 + 128],
                                xT[:, c * 512:(c + 1) * 512],
                                start=(c == 0), stop=(c == FC - 1)),
                                w=[(dsem, dma_t), bank_free[bank],
                                   S["ev"].cur() if (j, oc) == (0, 0) else (S["ev"], 0)]
                                if c == 0 else (),
                                i=[(S["mm"], 1)] if c == FC - 1 else ())
                        mm_t = S["mm"].n
                        if o_glob < 2048:
                            dstT, off = qT, o_glob // 128
                        elif o_glob < 4096:
                            dstT, off = kT, (o_glob - 2048) // 128
                        else:
                            dstT, off = vTt, (o_glob - 4096) // 128
                        p.o("scalar", lambda s, dstT=dstT, off=off, bank=bank: s.activation(
                            dstT[:, off * 512:(off + 1) * 512],
                            ps[:, bank * 512:(bank + 1) * 512], AF.Silu),
                            w=[(S["mm"], mm_t)], i=[(S["sc"], 1)])
                        bank_free[bank] = (S["sc"], S["sc"].n)
                    wfree[buf] = (S["mm"], S["mm"].n)
                sc_cur = S["sc"].n
                p.o("sync", lambda e, sb=sb: e.dma_start(
                    out=qts.ap().rearrange("h p m -> p h m")[:, :, sb * 512:(sb + 1) * 512],
                    in_=qT.ap().rearrange("p (h m) -> p h m", h=NH)),
                    w=[(S["sc"], sc_cur)], i=[(S["dS"], 16)])
                p.o("sync", lambda e: e.wait_ge(S["dS"].h, S["dS"].n) or None)
                p.close()

                for bi in range(2):
                    blk = sb * 2 + bi
                    moff = bi * 256
                    p = Ph(nc)
                    for h in range(NH):
                        for ncc in range(2):
                            p.o("scalar", lambda s, h=h, ncc=ncc: s.activation(
                                ddT[:, ncc * 256:(ncc + 1) * 256],
                                ddf[:, ncc * 256:(ncc + 1) * 256], AF.Exp,
                                scale=slopeb[:, h:h + 1]),
                                w=[S["ev"].cur(), S["gp"].cur()] if ncc == 0 else (),
                                i=[(S["sc"], 1)])
                        p.o("scalar", lambda s, h=h: s.activation(
                            qdec[:, :], iot256[:, :], AF.Exp, scale=nslopeb[:, h:h + 1]),
                            w=[S["ev"].cur()], i=[(S["sc"], 1)])
                        sc_qd = S["sc"].n
                        for ncc in range(2):
                            p.o("tensor", lambda t, h=h, ncc=ncc, moff=moff: t.matmul(
                                ps[:, ncc * 256:(ncc + 1) * 256],
                                kT[:, h * 512 + moff + ncc * 128:
                                   h * 512 + moff + (ncc + 1) * 128],
                                qT[:, h * 512 + moff: h * 512 + moff + 256],
                                start=True, stop=True),
                                w=[S["ev"].cur(), S["gp"].cur()] if ncc == 0 else (),
                                i=[(S["mm"], 1)] if ncc == 1 else ())
                        mm_sT = S["mm"].n
                        for ncc in range(2):
                            p.o("tensor", lambda t, h=h, ncc=ncc, moff=moff: t.transpose(
                                psb[:, ncc * 128:(ncc + 1) * 128],
                                kT[:, h * 512 + moff + ncc * 128:
                                   h * 512 + moff + (ncc + 1) * 128],
                                identb[:, :]),
                                w=[S["ev"].cur()] if ncc == 0 else ())
                            p.o("tensor", lambda t, h=h, ncc=ncc, moff=moff: t.transpose(
                                psb[:, 256 + ncc * 128: 256 + (ncc + 1) * 128],
                                vTt[:, h * 512 + moff + ncc * 128:
                                    h * 512 + moff + (ncc + 1) * 128],
                                identb[:, :]),
                                i=[(S["mm"], 1)] if ncc == 1 else ())
                        mm_kv = S["mm"].n
                        for ncc in range(2):
                            p.o("vector", lambda v, h=h, ncc=ncc: v.tensor_scalar_mul(
                                ks[:, ncc * 128:(ncc + 1) * 128],
                                psb[:, ncc * 128:(ncc + 1) * 128],
                                kdT[:, h * 2 + ncc:h * 2 + ncc + 1]),
                                w=[(S["mm"], mm_kv)] if ncc == 0 else (), i=[(S["ev"], 1)])
                            p.o("vector", lambda v, ncc=ncc: v.tensor_copy(
                                vtm[:, ncc * 128:(ncc + 1) * 128],
                                psb[:, 256 + ncc * 128: 256 + (ncc + 1) * 128]),
                                i=[(S["ev"], 1)])
                        for ncc in range(2):
                            p.o("vector", lambda v, ncc=ncc: v.tensor_tensor(
                                sdT[:, ncc * 256:(ncc + 1) * 256],
                                ps[:, ncc * 256:(ncc + 1) * 256],
                                ddT[:, ncc * 256:(ncc + 1) * 256], ALU.mult),
                                w=[(S["mm"], mm_sT), (S["sc"], sc_qd), S["gp"].cur()]
                                if ncc == 0 else (),
                                i=[(S["ev"], 1)])
                        p.o("vector", lambda v, h=h, moff=moff: v.tensor_tensor(
                            qd[:, :], qT[:, h * 512 + moff: h * 512 + moff + 256],
                            qdec[:, :], ALU.mult),
                            w=[(S["sc"], sc_qd)], i=[(S["ev"], 1)])
                        ev_ops = S["ev"].n
                        p.o("tensor", lambda t, h=h: t.matmul(
                            ps[:, 1024:1280], stb[:, h * 128:(h + 1) * 128], qd[:, :],
                            start=True, stop=False),
                            w=[(S["ev"], ev_ops), S["sc"].cur()])
                        for ncc in range(2):
                            p.o("tensor", lambda t, ncc=ncc: t.matmul(
                                ps[:, 1024:1280], vtm[:, ncc * 128:(ncc + 1) * 128],
                                sdT[:, ncc * 256:(ncc + 1) * 256],
                                start=False, stop=(ncc == 1)),
                                i=[(S["mm"], 1)] if ncc == 1 else ())
                        for ncc in range(2):
                            p.o("tensor", lambda t, ncc=ncc: t.matmul(
                                ps[:, 1536:1664], ks[:, ncc * 128:(ncc + 1) * 128],
                                vtm[:, ncc * 128:(ncc + 1) * 128],
                                start=(ncc == 0), stop=(ncc == 1)),
                                i=[(S["mm"], 1)] if ncc == 1 else ())
                        mm_o = S["mm"].n
                        p.o("scalar", lambda s, h=h: s.activation(
                            ost[:, h * 256:(h + 1) * 256], ps[:, 1024:1280], AF.Copy),
                            w=[(S["mm"], mm_o)], i=[(S["sc"], 1)])
                        p.o("vector", lambda v, h=h: v.scalar_tensor_tensor(
                            state[:, h * 128:(h + 1) * 128], state[:, h * 128:(h + 1) * 128],
                            bdb[:, h:h + 1], ps[:, 1536:1664], ALU.mult, ALU.add),
                            w=[(S["mm"], mm_o)], i=[(S["ev"], 1)])
                        ev_st = S["ev"].n
                        p.o("scalar", lambda s, h=h: s.activation(
                            stb[:, h * 128:(h + 1) * 128], state[:, h * 128:(h + 1) * 128],
                            AF.Copy), w=[(S["ev"], ev_st)], i=[(S["sc"], 1)])
                    sc_cur = S["sc"].n
                    p.o("sync", lambda e, blk=blk: e.dma_start(
                        out=obuf_d.ap().rearrange("h p m -> p h m")[:, :, blk * 256:(blk + 1) * 256],
                        in_=ost.ap().rearrange("p (h m) -> p h m", h=NH)),
                        w=[(S["sc"], sc_cur)], i=[(S["dS"], 16)])
                    p.o("sync", lambda e: e.wait_ge(S["dS"].h, S["dS"].n) or None)
                    p.close()

            p = Ph(nc)
            ev_cur = S["ev"].n
            p.o("sync", lambda e: e.dma_start(
                out=agin.ap().rearrange("h p e -> p h e"),
                in_=state.ap().rearrange("p (h e) -> p h e", h=NH)),
                w=[(S["ev"], ev_cur)], i=[(S["dS"], 16)])
            dma_t = S["dS"].n
            p.o("gpsimd", lambda g: g.collective_compute(
                "AllGather", ALU.bypass,
                replica_groups=[[0, 1, 2, 3], [4, 5, 6, 7]],
                ins=[agin.ap().opt()], outs=[agout.ap().opt()]),
                w=[(S["dS"], dma_t)], i=[(S["cc"], 1)])
            p.close()

        with contextlib.ExitStack() as st:
            kvin = st.enter_context(nc.sbuf_tensor("ab_kv", [128, NH * 128], F32))
            kvb = st.enter_context(nc.sbuf_tensor("ab_kvb", [128, NH * 128], BF16))
            aag = st.enter_context(nc.sbuf_tensor("ab_aag", [128, NH * 128], F32))
            wjb = st.enter_context(nc.sbuf_tensor("ab_wjb", [128, 48], F32))
            ebm = st.enter_context(nc.sbuf_tensor("ab_ebm", [128, 8], F32))
            obh = st.enter_context(nc.sbuf_tensor("ab_obh", [128, 1024], BF16))
            qth = st.enter_context(nc.sbuf_tensor("ab_qth", [128, 1024], BF16))
            iot1k = st.enter_context(nc.sbuf_tensor("ab_iot1k", [128, 1024], F32))
            cdb = st.enter_context(nc.sbuf_tensor("ab_cdb", [128, 1024], F32))
            qs = st.enter_context(nc.sbuf_tensor("ab_qs", [128, 1024], BF16))
            otot = st.enter_context(nc.sbuf_tensor("ab_ot", [128, 1024], BF16))
            sqh = st.enter_context(nc.sbuf_tensor("ab_sq", [128, 1024], BF16))
            god = st.enter_context(nc.sbuf_tensor("ab_god", [128, NH * 1024], BF16))
            xT2 = st.enter_context(nc.sbuf_tensor("ab_xT2", [128, FC * 1024], BF16))
            xlnb2 = st.enter_context(nc.sbuf_tensor("ab_xlnb2", [128, D], BF16))
            xlnb2b = st.enter_context(nc.sbuf_tensor("ab_xlnb2b", [128, D], BF16))
            gts = st.enter_context(nc.sbuf_tensor("ab_gts", [128, 1024], BF16))
            vrow = st.enter_context(nc.sbuf_tensor("ab_vrow", [1, 1024], F32))
            rstb = st.enter_context(nc.sbuf_tensor("ab_rstb", [128, 1024], F32))
            wbuf2 = st.enter_context(nc.sbuf_tensor("ab_w2", [128, FC * 512], BF16))
            ps = st.enter_context(nc.psum_tensor("ab_ps2", [128, 3072], F32))
            psb = st.enter_context(nc.psum_tensor("ab_psb2", [128, 2048], BF16))

            p = Ph(nc)
            dma_t = S["dS"].n + 32
            p.o("sync", lambda e: e.dma_start(out=ebm[:, 0:3], in_=agw_e[:, :]),
                i=[(S["dS"], 16)])
            p.o("sync", lambda e: e.dma_start(out=ebm[:, 4:7], in_=agw_m[:, :]),
                i=[(S["dS"], 16)])
            p.o("gpsimd", lambda g: g.engine_nop(), w=[(S["dS"], dma_t)], i=[(S["gp"], 1)])
            gp_t = S["gp"].n
            p.o("gpsimd", lambda g: g.iota(iot1k[:, :], [[1, 1024]], base=1,
                                           channel_multiplier=0,
                                           allow_small_or_imprecise_dtypes=True),
                i=[(S["gp"], 1)])
            for h in range(NH):
                p.o("vector", lambda v, h=h: v.tensor_scalar_mul(
                    wjb[:, h * 3:(h + 1) * 3], ebm[:, 0:3], slopeb[:, h:h + 1]),
                    w=[(S["gp"], gp_t), (S["dS"], dma_t)] if h == 0 else (), i=[(S["ev"], 1)])
            ev_t = S["ev"].n
            for h in range(NH):
                p.o("scalar", lambda s, h=h: s.activation(
                    wjb[:, h * 3:(h + 1) * 3], wjb[:, h * 3:(h + 1) * 3], AF.Exp,
                    scale=-1024.0),
                    w=[(S["ev"], ev_t)] if h == 0 else (), i=[(S["sc"], 1)])
            sc_t = S["sc"].n
            for h in range(NH):
                p.o("vector", lambda v, h=h: v.tensor_mul(
                    wjb[:, h * 3:(h + 1) * 3], wjb[:, h * 3:(h + 1) * 3], ebm[:, 4:7]),
                    w=[(S["sc"], sc_t)] if h == 0 else (), i=[(S["ev"], 1)])
            p.o("gpsimd", lambda g: g.memset(kvin[:, :], 0.0), i=[(S["gp"], 1)])
            gp_t = S["gp"].n
            cc_t = S["cc"].n
            for j in range(3):
                dma_t = S["dS"].n + 16
                p.o("sync", lambda e, j=j: e.dma_start(
                    out=aag.ap().rearrange("p (h e) -> p h e", h=NH),
                    in_=agout.ap()[j].rearrange("h p e -> p h e")),
                    w=[(S["cc"], cc_t), S["ev"].cur()], i=[(S["dS"], 16)])
                for h in range(NH):
                    p.o("vector", lambda v, h=h, j=j: v.scalar_tensor_tensor(
                        kvin[:, h * 128:(h + 1) * 128], aag[:, h * 128:(h + 1) * 128],
                        wjb[:, h * 3 + j:h * 3 + j + 1], kvin[:, h * 128:(h + 1) * 128],
                        ALU.mult, ALU.add),
                        w=[(S["dS"], dma_t), (S["gp"], gp_t)], i=[(S["ev"], 1)])
            ev_kv = S["ev"].n
            p.o("scalar", lambda s: s.activation(kvb[:, :], kvin[:, :], AF.Copy),
                w=[(S["ev"], ev_kv)], i=[(S["sc"], 1)])
            p.close()

            p = Ph(nc)
            sc_kvb = S["sc"].n
            for h in range(NH):
                dma_o = S["dS"].n + 16
                p.o("sync", lambda e, h=h: e.dma_start(out=obh[:, :], in_=obuf_d.ap()[h]),
                    w=[S["ev"].cur()], i=[(S["dS"], 16)])
                dma_q = S["dS"].n + 16
                p.o("sync", lambda e, h=h: e.dma_start(out=qth[:, :], in_=qts.ap()[h]),
                    w=[S["gp"].cur()], i=[(S["dS"], 16)])
                p.o("scalar", lambda s, h=h: s.activation(
                    cdb[:, :], iot1k[:, :], AF.Exp, scale=nslopeb[:, h:h + 1]),
                    w=[S["gp"].cur()], i=[(S["sc"], 1)])
                sc_cd = S["sc"].n
                p.o("gpsimd", lambda g: g.tensor_mul(qs[:, :], qth[:, :], cdb[:, :]),
                    w=[(S["dS"], dma_q), (S["mm"], S["mm"].n), (S["sc"], sc_cd)],
                    i=[(S["gp"], 1)])
                gp_qs = S["gp"].n
                for mh in range(2):
                    p.o("tensor", lambda t, h=h, mh=mh: t.matmul(
                        ps[:, 1024 + mh * 512:1024 + (mh + 1) * 512],
                        kvb[:, h * 128:(h + 1) * 128], qs[:, mh * 512:(mh + 1) * 512],
                        start=True, stop=True),
                        w=[(S["gp"], gp_qs), (S["sc"], sc_kvb), S["ev"].cur()]
                        if mh == 0 else (),
                        i=[(S["mm"], 1)] if mh == 1 else ())
                mm_t = S["mm"].n
                p.o("vector", lambda v: v.tensor_tensor(
                    otot[:, :], obh[:, :], ps[:, 1024:2048], ALU.add),
                    w=[(S["mm"], mm_t), (S["dS"], dma_o), S["sc"].cur(),
                       (S["dS"], S["dS"].n)], i=[(S["ev"], 1)])
                ev_ot = S["ev"].n
                p.o("sync", lambda e, h=h: e.dma_start(out=obuf_d.ap()[h], in_=otot[:, :]),
                    w=[(S["ev"], ev_ot)], i=[(S["dS"], 16)])
                p.o("scalar", lambda s: s.activation(sqh[:, :], otot[:, :], AF.Square),
                    w=[(S["ev"], ev_ot)], i=[(S["sc"], 1)])
                sc_t = S["sc"].n
                for mh in range(2):
                    p.o("tensor", lambda t, h=h, mh=mh: t.matmul(
                        ps[0:1, 2048 + mh * 512:2048 + (mh + 1) * 512],
                        onesb[:, 0:1], sqh[:, mh * 512:(mh + 1) * 512],
                        start=(h == 0), stop=(h == NH - 1)),
                        w=[(S["sc"], sc_t)] if mh == 0 else (),
                        i=[(S["mm"], 1)] if (mh == 1 and h == NH - 1) else ())
            mm_var = S["mm"].n
            p.o("sync", lambda e: e.wait_ge(S["dS"].h, S["dS"].n) or None)
            p.o("vector", lambda v: v.tensor_scalar(
                vrow[0:1, :], ps[0:1, 2048:3072], 1.0 / D, EPS_RMS, ALU.mult, ALU.add),
                w=[(S["mm"], mm_var)], i=[(S["ev"], 1)])
            p.o("vector", lambda v: v.reciprocal(vrow[0:1, :], vrow[0:1, :]),
                w=[S["ev"].cur()], i=[(S["ev"], 1)])
            ev_t = S["ev"].n
            p.o("scalar", lambda s: s.activation(vrow[0:1, :], vrow[0:1, :], AF.Sqrt),
                w=[(S["ev"], ev_t)], i=[(S["sc"], 1)])
            sc_t = S["sc"].n
            for mh in range(2):
                p.o("tensor", lambda t, mh=mh: t.matmul(
                    ps[:, mh * 512:(mh + 1) * 512], onesrf[0:1, :],
                    vrow[0:1, mh * 512:(mh + 1) * 512], start=True, stop=True),
                    w=[(S["sc"], sc_t)] if mh == 0 else (),
                    i=[(S["mm"], 1)] if mh == 1 else ())
            mm_t = S["mm"].n
            p.o("vector", lambda v: v.tensor_copy(rstb[:, :], ps[:, 0:1024]),
                w=[(S["mm"], mm_t)], i=[(S["ev"], 1)])
            p.o("gpsimd", lambda g: g.engine_nop(), w=[S["ev"].cur()], i=[(S["gp"], 1)])
            p.close()

            p = Ph(nc)
            srcs = [R[:, wt * D:(wt + 1) * D] for wt in range(8)]
            ln_xt_build(p, srcs, xT2, 1024, psb, [xlnb2, xlnb2b], t0=0)
            p.close()

            p = Ph(nc)
            bank_free = {b: (S["sc"], 0) for b in range(2)}
            wfree = (S["mm"], 0)
            gts_free = (S["gp"], 0)
            otot_free = (S["gp"], 0)
            for j in range(4):
                dma_t = S["dG"].n + 16
                p.o("gpsimd", lambda e, j=j: e.dma_start(
                    out=wbuf2.ap().rearrange("p (c o) -> p c o", c=FC),
                    in_=wg_d.ap()[j]),
                    w=[wfree], i=[(S["dG"], 16)])
                for oc in range(4):
                    h = j * 4 + oc
                    bank = oc % 2
                    for mh in range(2):
                        for c in range(FC):
                            p.o("tensor", lambda t, c=c, oc=oc, mh=mh, bank=bank: t.matmul(
                                ps[:, bank * 1024 + mh * 512: bank * 1024 + (mh + 1) * 512],
                                wbuf2[:, c * 512 + oc * 128: c * 512 + oc * 128 + 128],
                                xT2[:, c * 1024 + mh * 512: c * 1024 + (mh + 1) * 512],
                                start=(c == 0), stop=(c == FC - 1)),
                                w=[(S["dG"], dma_t), bank_free[bank],
                                   S["ev"].cur() if (j, oc) == (0, 0) else (S["ev"], 0)]
                                if c == 0 else (),
                                i=[(S["mm"], 1)] if c == FC - 1 else ())
                    mm_t = S["mm"].n
                    p.o("scalar", lambda s, bank=bank: s.activation(
                        gts[:, :], ps[:, bank * 1024:(bank + 1) * 1024], AF.Sigmoid),
                        w=[(S["mm"], mm_t), gts_free], i=[(S["sc"], 1)])
                    bank_free[bank] = (S["sc"], S["sc"].n)
                    sc_t = S["sc"].n
                    dma_o = S["dS"].n + 16
                    p.o("sync", lambda e, h=h: e.dma_start(out=obh[:, :], in_=obuf_d.ap()[h]),
                        w=[S["ev"].cur()], i=[(S["dS"], 16)])
                    p.o("vector", lambda v, h=h: v.scalar_tensor_tensor(
                        otot[:, :], obh[:, :], normw_s[:, h:h + 1], rstb[:, :],
                        ALU.mult, ALU.mult),
                        w=[(S["dS"], dma_o), otot_free, S["gp"].cur()], i=[(S["ev"], 1)])
                    ev_t = S["ev"].n
                    p.o("gpsimd", lambda g, h=h: g.tensor_mul(
                        god[:, h * 1024:(h + 1) * 1024], gts[:, :], otot[:, :]),
                        w=[(S["ev"], ev_t), (S["sc"], sc_t)], i=[(S["gp"], 1)])
                    gts_free = (S["gp"], S["gp"].n)
                    otot_free = (S["gp"], S["gp"].n)
                wfree = (S["mm"], S["mm"].n)
            p.close()

            p = Ph(nc)
            gp_god = S["gp"].n
            bank_free = {b: (S["ev"], 0) for b in range(4)}
            wfree = (S["mm"], 0)
            for fb in range(4):
                dma_t = S["dG"].n + 16
                p.o("gpsimd", lambda e, fb=fb: e.dma_start(
                    out=wbuf2.ap().rearrange("p (c o) -> p c o", c=FC),
                    in_=wo2_d.ap()[fb]),
                    w=[wfree], i=[(S["dG"], 16)])
                for qt in range(8):
                    bank = qt % 4
                    for h in range(NH):
                        p.o("tensor", lambda t, h=h, qt=qt, bank=bank: t.matmul(
                            ps[:, bank * 512:(bank + 1) * 512],
                            god[:, h * 1024 + qt * 128: h * 1024 + (qt + 1) * 128],
                            wbuf2[:, h * 512:(h + 1) * 512],
                            start=(h == 0), stop=(h == NH - 1)),
                            w=[(S["dG"], dma_t), (S["gp"], gp_god), bank_free[bank]]
                            if h == 0 else (),
                            i=[(S["mm"], 1)] if h == NH - 1 else ())
                    mm_t = S["mm"].n
                    p.o("vector", lambda v, qt=qt, fb=fb, bank=bank: v.tensor_tensor(
                        R[:, qt * D + fb * 512: qt * D + (fb + 1) * 512],
                        R[:, qt * D + fb * 512: qt * D + (fb + 1) * 512],
                        ps[:, bank * 512:(bank + 1) * 512], ALU.add),
                        w=[(S["mm"], mm_t)], i=[(S["ev"], 1)])
                    bank_free[bank] = (S["ev"], S["ev"].n)
                wfree = (S["mm"], S["mm"].n)
            p.close()

    if n_sub >= 2:
        ffn_ternarize()
    if n_sub >= 1:
        attn_a()
    if n_sub >= 2:
        ffn(0)
    if n_sub >= 3:
        attn_b()
    if n_sub >= 4:
        ffn(1)

    p = Ph(nc)
    if n_sub < 4:
        ev_cur = S["ev"].n
        p.o("sync", lambda e: e.dma_start(
            out=y_out.ap().rearrange("(t p) f -> p t f", p=128),
            in_=R.ap().rearrange("p (t f) -> p t f", t=NT)),
            w=[(S["ev"], ev_cur)], i=[(S["out"], 16)])
    p.o("sync", lambda e: e.wait_ge(S["out"].h, S["out"].n) or None)
    p.close()

    return nc


def _blk_fco(w):
    w = np.asarray(w, np.float32)
    F, O = w.shape
    v = w.reshape(F // 128, 128, O // 512, 512)
    return np.ascontiguousarray(v.transpose(2, 1, 0, 3))


def _prep_inputs(inputs):
    T = np.ascontiguousarray
    d = {}
    d["wq"] = _blk_fco(inputs["L0_attn_a_Wq"])
    d["wk"] = _blk_fco(inputs["L0_attn_a_Wk"])
    d["wv"] = _blk_fco(inputs["L0_attn_a_Wv"])
    d["wo"] = _blk_fco(inputs["L0_attn_a_Wo"])
    w = np.asarray(inputs["L0_attn_b_w_qkv"], np.float32).reshape(16, 3, 128, 2048)
    d["wqkv_r"] = _blk_fco(T(w.transpose(1, 0, 2, 3).reshape(6144, 2048).T))
    d["wg_t"] = _blk_fco(np.asarray(inputs["L0_attn_b_w_gate"], np.float32).T)
    d["wout2"] = _blk_fco(np.asarray(inputs["L0_attn_b_w_out"], np.float32).T)
    d["normw"] = inputs["L0_attn_b_norm_w"]
    d["slope"] = T(np.broadcast_to(
        np.asarray(inputs["L0_attn_b_slope_rate"], np.float32).reshape(1, 16), (128, 16)))
    d["ob_r"] = T(np.broadcast_to(
        np.asarray(inputs["L0_ffn_out_b"], np.float32).reshape(1, 2048), (128, 2048)))
    d["gate_b"] = inputs["L0_ffn_gate_b"]
    d["val_b"] = inputs["L0_ffn_val_b"]
    d["out_b"] = inputs["L0_ffn_out_b"]
    bg = _blk_fco(np.asarray(inputs["L0_ffn_gate_w"], np.float32).T)
    bv = _blk_fco(np.asarray(inputs["L0_ffn_val_w"], np.float32).T)
    wof = np.asarray(inputs["L0_ffn_out_w"], np.float32).T
    v = wof.reshape(64, 128, 4, 512)
    bo = np.ascontiguousarray(v.transpose(2, 0, 1, 3))
    return d, bg, bv, bo.reshape(256, 128, 512)


def make_in_maps(inputs):
    T = np.ascontiguousarray
    shared, bg, bv, bo = _prep_inputs(inputs)
    x = inputs["L0_attn_a_x"]
    in_maps = []
    for c in range(8):
        b, pc = c // 4, c % 4
        m = dict(shared)
        m["wg_sh"] = T(bg[2 * c:2 * c + 2])
        m["wv_sh"] = T(bv[2 * c:2 * c + 2])
        m["wo_sh"] = T(bo[32 * c:32 * c + 32])
        m["x_own"] = T(np.asarray(x[b, pc * 1024:(pc + 1) * 1024], np.float32))
        if pc == 0:
            m["x_prev"] = np.zeros((512, D), np.float32)
            m["pmrow"] = np.full((1, 512), NEG, np.float32)
        else:
            m["x_prev"] = T(np.asarray(x[b, pc * 1024 - 512:pc * 1024], np.float32))
            m["pmrow"] = np.zeros((1, 512), np.float32)
        e = np.zeros((128, 3), np.float32)
        mk = np.zeros((128, 3), np.float32)
        for j in range(3):
            if j < pc:
                e[:, j] = pc - 1 - j
                mk[:, j] = 1.0
        m["agw_e"] = e
        m["agw_m"] = mk
        in_maps.append(m)
    return in_maps


def kernel(**inputs):
    in_maps = make_in_maps(inputs)
    nc = build_program(4)
    res = run_bass_kernel_spmd(nc, in_maps, core_ids=list(range(8)))
    out = np.zeros((2, 4096, D), np.float32)
    for c in range(8):
        b, pc = c // 4, c % 4
        out[b, pc * 1024:(pc + 1) * 1024] = res.results[c]["y"]
    return out
